# version 06
# baseline (speedup 1.0000x reference)
import contextlib
import numpy as np
import concourse.bass as bass
import concourse.mybir as mybir
from concourse.bass_utils import run_bass_kernel_spmd

F32 = mybir.dt.float32
BF16 = mybir.dt.bfloat16
AF = mybir.ActivationFunctionType
ALU = mybir.AluOpType
AX = mybir.AxisListType

P = 128
FC = 16
D = 2048
T = 2048
PRE = 64
W = 2176
ICOL = 2112
QSCALE = float(1.0 / np.sqrt(128.0))
SHIFT = -20.0
INVD = float(1.0 / 2048.0)
EPS = 1e-5
NEXP = 8
MOE_I = 1408
SH_I = 5632
CC_ALL = [(0, 512), (512, 512), (1024, 512), (1536, 512), (2048, 128)]
CC_MAIN = [(0, 512), (512, 512), (1024, 512), (1536, 512)]

import os
PHASES = int(os.environ.get("KPH", "99"))
KDBG = os.environ.get("KDBG", "")
KBLK = int(os.environ.get("KBLK", "4"))
KSKIP = os.environ.get("KSKIP", "")
KSEC = os.environ.get("KSEC", "abc")
KP6 = os.environ.get("KP6", "sg,gu")
KNJ = int(os.environ.get("KNJ", "44"))


class SemT:
    def __init__(self, nc, st, name):
        self.h = st.enter_context(nc.semaphore(name))
        self.n = 0

    def inc(self, instr, k=1):
        instr.then_inc(self.h, k)
        self.n += k
        return self.n


def wait_all(eng, sems, snap):
    for s in sems:
        if snap[s] > 0:
            eng.wait_ge(s.h, snap[s])


def build_nc():
    nc = bass.Bass("TRN2", target_bir_lowering=False)

    x_slab = nc.dram_tensor("x_slab", [D, W], F32, kind="ExternalInput").ap()
    wq_t = nc.dram_tensor("wq_t", [D, D], F32, kind="ExternalInput").ap()
    wk_t = nc.dram_tensor("wk_t", [D, 1024], F32, kind="ExternalInput").ap()
    wv_t = nc.dram_tensor("wv_t", [D, 1024], F32, kind="ExternalInput").ap()
    wo_t = nc.dram_tensor("wo_t", [D, D], F32, kind="ExternalInput").ap()
    gate_pt = nc.dram_tensor("gate_pt", [P, FC, 8], F32, kind="ExternalInput").ap()
    sg_pt = nc.dram_tensor("sg_pt", [P, FC, 1], F32, kind="ExternalInput").ap()
    gu_t = nc.dram_tensor("gu_t", [D, 2 * SH_I], F32, kind="ExternalInput").ap()
    dn_t = nc.dram_tensor("dn_t", [SH_I, D], F32, kind="ExternalInput").ap()
    w13_t = nc.dram_tensor("w13_t", [NEXP, D, 2 * MOE_I], F32, kind="ExternalInput").ap()
    w2_t = nc.dram_tensor("w2_t", [NEXP, MOE_I, D], F32, kind="ExternalInput").ap()
    maskF = nc.dram_tensor("maskF", [P, 256], F32, kind="ExternalInput").ap()
    maskM = nc.dram_tensor("maskM", [P, 256], F32, kind="ExternalInput").ap()
    maskFI = nc.dram_tensor("maskFI", [4, P], F32, kind="ExternalInput").ap()
    maskMI = nc.dram_tensor("maskMI", [4, P], F32, kind="ExternalInput").ap()
    eye_in = nc.dram_tensor("eye_in", [P, P], F32, kind="ExternalInput").ap()

    out = nc.dram_tensor("out", [D, T], F32, kind="ExternalOutput").ap()
    dbg = nc.dram_tensor("dbg", [17 * P, 1024], F32, kind="ExternalOutput").ap()
    ao_scr = nc.dram_tensor("ao_scr", [D, T], BF16, kind="Internal").ap()
    hs_scr = nc.dram_tensor("hs_scr", [SH_I, T], BF16, kind="Internal").ap()

    st = contextlib.ExitStack()
    with st:
        sQD = SemT(nc, st, "sQD")
        sHD = SemT(nc, st, "sHD")
        sSD = SemT(nc, st, "sSD")
        sPE = SemT(nc, st, "sPE")
        sDV = SemT(nc, st, "sDV")
        sAC = SemT(nc, st, "sAC")
        ALL = [sQD, sHD, sSD, sPE, sDV, sAC]

        def snap():
            return {s: s.n for s in ALL}

        def qdma(eng, sem, dst, srcap):
            if sem.n > 0:
                eng.wait_ge(sem.h, sem.n)
            eng.dma_start(dst, srcap).then_inc(sem.h, 16)
            sem.n += 16

        PB = [st.enter_context(nc.psum_tensor(f"PB{i}", [P, 512], F32)) for i in range(8)]

        ones_bf = st.enter_context(nc.sbuf_tensor("ones_bf", [P, 1], BF16))
        ones1_f = st.enter_context(nc.sbuf_tensor("ones1_f", [1, P], F32))
        maskF_sb = st.enter_context(nc.sbuf_tensor("maskF_sb", [P, 2, P], BF16))
        maskM_sb = st.enter_context(nc.sbuf_tensor("maskM_sb", [P, 2, P], BF16))
        maskFI_sb = st.enter_context(nc.sbuf_tensor("maskFI_sb", [4, P], BF16))
        maskMI_sb = st.enter_context(nc.sbuf_tensor("maskMI_sb", [4, P], BF16))
        eye_sb = st.enter_context(nc.sbuf_tensor("eye_sb", [P, P], F32))
        gate_sb = st.enter_context(nc.sbuf_tensor("gate_sb", [P, FC, 8], F32))
        sgw_sb = st.enter_context(nc.sbuf_tensor("sgw_sb", [P, FC, 1], BF16))
        shift_sb = st.enter_context(nc.sbuf_tensor("shift_sb", [P, 1], F32))

        block = st.enter_context(nc.Block())

        @block.gpsimd
        def _(g):
            qdma(g, sQD, maskF_sb[:, :, :], maskF)
            qdma(g, sQD, maskM_sb[:, :, :], maskM)
            qdma(g, sQD, maskFI_sb[:, :], maskFI)
            qdma(g, sQD, maskMI_sb[:, :], maskMI)
            qdma(g, sQD, eye_sb[:, :], eye_in)
            qdma(g, sQD, gate_sb[:, :, :], gate_pt)
            qdma(g, sQD, sgw_sb[:, :, :], sg_pt)

        @block.vector
        def _(v):
            v.memset(ones_bf[:, :], 1.0)
            v.memset(shift_sb[:, :], SHIFT)
            v.memset(ones1_f[:, :], 1.0).then_inc(sDV.h, 1)
        sDV.n += 1

        s13 = st.enter_context(contextlib.ExitStack())
        xn1 = s13.enter_context(nc.sbuf_tensor("xn1", [P, FC, W], BF16))

        if PHASES >= 1:
            s1 = contextlib.ExitStack()
            with s1:
                xin = [s1.enter_context(nc.sbuf_tensor(f"xin{i}", [P, FC, 512], F32)) for i in range(2)]
                xbf = s1.enter_context(nc.sbuf_tensor("xbf", [P, FC, 512], BF16))
                xsq = s1.enter_context(nc.sbuf_tensor("xsq", [P, FC, 512], BF16))
                t1 = s1.enter_context(nc.sbuf_tensor("t1", [P, 512], F32))
                rows = [s1.enter_context(nc.sbuf_tensor(f"r1_{k}", [1, 2, 512], F32))
                        for k in range(6)]
                Rm, Rq, Rmsq, Rsd, Rr, Rmr = rows
                b = snap()
                NC = len(CC_ALL)

                @block.sync
                def _(s):
                    for i, (c0, wn) in enumerate(CC_ALL):
                        if i >= 2:
                            s.wait_ge(sDV.h, b[sDV] + 3 * (i - 2) + 3)
                        src = x_slab[:, c0:c0 + wn].rearrange("(c p) t -> p c t", p=P)
                        qdma(s, sHD, xin[i % 2][:, :, 0:wn], src)

                @block.gpsimd
                def _(g):
                    for i, (c0, wn) in enumerate(CC_ALL):
                        if i >= 1:
                            g.wait_ge(sPE.h, b[sPE] + 2 * (i - 1) + 1)
                        src = x_slab[:, c0:c0 + wn].rearrange("(c p) t -> p c t", p=P)
                        qdma(g, sQD, xbf[:, :, 0:wn], src)

                @block.scalar
                def _(sc):
                    for i, (c0, wn) in enumerate(CC_ALL):
                        sc.wait_ge(sHD.h, b[sHD] + 16 * (i + 1))
                        if i >= 1:
                            sc.wait_ge(sPE.h, b[sPE] + 2 * (i - 1) + 1)
                        sc.activation(xsq[:, :, 0:wn], xin[i % 2][:, :, 0:wn], AF.Square).then_inc(sAC.h, 1)
                        sc.wait_ge(sDV.h, b[sDV] + 3 * i + 1)
                        sc.activation(Rsd[:, i % 2, 0:wn], Rq[:, i % 2, 0:wn], AF.Sqrt).then_inc(sAC.h, 1)
                sAC.n += 2 * NC

                @block.tensor
                def _(te):
                    te.wait_ge(sDV.h, b[sDV])
                    for i, (c0, wn) in enumerate(CC_ALL):
                        te.wait_ge(sQD.h, b[sQD] + 16 * (i + 1))
                        te.wait_ge(sAC.h, b[sAC] + 2 * i + 1)
                        if i >= 1:
                            te.wait_ge(sDV.h, b[sDV] + 3 * (i - 1) + 1)
                        for fc in range(FC):
                            te.matmul(PB[0][0:1, 0:wn], ones_bf[:, :], xbf[:, fc, 0:wn],
                                      start=(fc == 0), stop=(fc == FC - 1))
                        for fc in range(FC):
                            mm = te.matmul(PB[1][0:1, 0:wn], ones_bf[:, :], xsq[:, fc, 0:wn],
                                           start=(fc == 0), stop=(fc == FC - 1))
                        mm.then_inc(sPE.h, 1)
                        te.wait_ge(sDV.h, b[sDV] + 3 * i + 2)
                        if i >= 2:
                            te.wait_ge(sDV.h, b[sDV] + 3 * (i - 2) + 3)
                        te.matmul(PB[2 + i % 2][:, 0:wn], ones1_f[:, :], Rr[:, i % 2, 0:wn],
                                  start=True, stop=True)
                        mm = te.matmul(PB[4 + i % 2][:, 0:wn], ones1_f[:, :], Rmr[:, i % 2, 0:wn],
                                       start=True, stop=True)
                        mm.then_inc(sPE.h, 1)
                sPE.n += 2 * NC

                @block.vector
                def _(v):
                    for i, (c0, wn) in enumerate(CC_ALL):
                        ip = i % 2
                        v.wait_ge(sPE.h, b[sPE] + 2 * i + 1)
                        if i >= 2:
                            v.wait_ge(sAC.h, b[sAC] + 2 * (i - 2) + 2)
                            v.wait_ge(sPE.h, b[sPE] + 2 * (i - 2) + 2)
                        v.tensor_scalar(Rm[:, ip, 0:wn], PB[0][0:1, 0:wn], INVD, None, op0=ALU.mult)
                        v.tensor_scalar(Rq[:, ip, 0:wn], PB[1][0:1, 0:wn], INVD, None, op0=ALU.mult)
                        v.tensor_tensor(Rmsq[:, ip, 0:wn], Rm[:, ip, 0:wn], Rm[:, ip, 0:wn], op=ALU.mult)
                        v.tensor_tensor(Rq[:, ip, 0:wn], Rq[:, ip, 0:wn], Rmsq[:, ip, 0:wn],
                                        op=ALU.subtract)
                        v.tensor_scalar(Rq[:, ip, 0:wn], Rq[:, ip, 0:wn], EPS, None,
                                        op0=ALU.add).then_inc(sDV.h, 1)
                        v.wait_ge(sAC.h, b[sAC] + 2 * i + 2)
                        v.reciprocal(Rr[:, ip, 0:wn], Rsd[:, ip, 0:wn])
                        v.tensor_tensor(Rmr[:, ip, 0:wn], Rm[:, ip, 0:wn], Rr[:, ip, 0:wn],
                                        op=ALU.mult).then_inc(sDV.h, 1)
                        v.wait_ge(sPE.h, b[sPE] + 2 * i + 2)
                        for fc in range(FC):
                            v.tensor_tensor(t1[:, 0:wn], xin[ip][:, fc, 0:wn], PB[2 + ip][:, 0:wn], op=ALU.mult)
                            tt = v.tensor_tensor(xn1[:, fc, c0:c0 + wn], t1[:, 0:wn], PB[4 + ip][:, 0:wn],
                                                 op=ALU.subtract)
                        tt.then_inc(sDV.h, 1)
                sDV.n += 3 * NC

        if PHASES >= 2:
            k_sb = s13.enter_context(nc.sbuf_tensor("k_sb", [P, 8, W], BF16))
            vT_sb = s13.enter_context(nc.sbuf_tensor("vT_sb", [P, 17, 1024], BF16))
            vI_sb = s13.enter_context(nc.sbuf_tensor("vI_sb", [4, 1024], BF16))
            s2 = contextlib.ExitStack()
            with s2:
                wkbuf = [s2.enter_context(nc.sbuf_tensor(f"wkbuf{i}", [P, FC, P], BF16)) for i in range(2)]
                wvbuf = [s2.enter_context(nc.sbuf_tensor(f"wvbuf{i}", [P, FC, 512], BF16)) for i in range(2)]
                b = snap()
                NCC = len(CC_ALL)

                @block.gpsimd
                def _(g):
                    for ock in range(8):
                        if ock >= 2:
                            g.wait_ge(sPE.h, b[sPE] + NCC * (ock - 1))
                        else:
                            g.wait_ge(sPE.h, b[sPE])
                        src = wk_t[:, ock * P:(ock + 1) * P].rearrange("(c p) n -> p c n", p=P)
                        qdma(g, sQD, wkbuf[ock % 2][:, :, :], src)
                    for on in range(2):
                        if on >= 2:
                            g.wait_ge(sPE.h, b[sPE] + NCC * 8 + 18 * (on - 1))
                        src = wv_t[:, on * 512:(on + 1) * 512].rearrange("(c p) n -> p c n", p=P)
                        qdma(g, sQD, wvbuf[on % 2][:, :, :], src)

                @block.tensor
                def _(te):
                    wait_all(te, [sDV, sQD], b)
                    gi = 0
                    for ock in range(8):
                        te.wait_ge(sQD.h, b[sQD] + 16 * (ock + 1))
                        for cc, (c0, wn) in enumerate(CC_ALL):
                            par = gi % 2
                            if gi >= 2:
                                te.wait_ge(sDV.h, b[sDV] + gi - 1)
                            for fc in range(FC):
                                mm = te.matmul(PB[par][:, 0:wn], wkbuf[ock % 2][:, fc, :],
                                               xn1[:, fc, c0:c0 + wn],
                                               start=(fc == 0), stop=(fc == FC - 1))
                            mm.then_inc(sPE.h, 1)
                            gi += 1
                    for on in range(2):
                        te.wait_ge(sQD.h, b[sQD] + 16 * (9 + on))
                        for tt in range(17):
                            par = gi % 2
                            if gi >= 2:
                                te.wait_ge(sDV.h, b[sDV] + gi - 1)
                            for fc in range(FC):
                                mm = te.matmul(PB[par][:, :], xn1[:, fc, tt * P:(tt + 1) * P],
                                               wvbuf[on % 2][:, fc, :],
                                               start=(fc == 0), stop=(fc == FC - 1))
                            mm.then_inc(sPE.h, 1)
                            gi += 1
                        par = gi % 2
                        if gi >= 2:
                            te.wait_ge(sDV.h, b[sDV] + gi - 1)
                        for fc in range(FC):
                            mm = te.matmul(PB[par][0:4, :], xn1[:, fc, ICOL:ICOL + 4],
                                           wvbuf[on % 2][:, fc, :],
                                           start=(fc == 0), stop=(fc == FC - 1))
                        mm.then_inc(sPE.h, 1)
                        gi += 1
                sPE.n += 8 * NCC + 2 * 18

                @block.vector
                def _(v):
                    gi = 0
                    for ock in range(8):
                        for cc, (c0, wn) in enumerate(CC_ALL):
                            v.wait_ge(sPE.h, b[sPE] + gi + 1)
                            v.tensor_copy(k_sb[:, ock, c0:c0 + wn], PB[gi % 2][:, 0:wn]).then_inc(sDV.h, 1)
                            gi += 1
                    for on in range(2):
                        for tt in range(17):
                            v.wait_ge(sPE.h, b[sPE] + gi + 1)
                            v.tensor_copy(vT_sb[:, tt, on * 512:(on + 1) * 512],
                                          PB[gi % 2][:, :]).then_inc(sDV.h, 1)
                            gi += 1
                        v.wait_ge(sPE.h, b[sPE] + gi + 1)
                        v.tensor_copy(vI_sb[0:4, on * 512:(on + 1) * 512],
                                      PB[gi % 2][0:4, :]).then_inc(sDV.h, 1)
                        gi += 1
                sDV.n += 8 * NCC + 2 * 18

        if PHASES >= 3:
            s3 = contextlib.ExitStack()
            with s3:
                wqg = [s3.enter_context(nc.sbuf_tensor(f"wqg{i}", [P, FC, 256], BF16)) for i in range(2)]
                q_blk = s3.enter_context(nc.sbuf_tensor("q_blk", [P, FC, 512], BF16))
                av_blk = s3.enter_context(nc.sbuf_tensor("av_blk", [P, FC, 512], BF16))
                pt_sb = [s3.enter_context(nc.sbuf_tensor(f"pt{i}", [P, 2, P], BF16)) for i in range(2)]
                ptI_sb = [s3.enter_context(nc.sbuf_tensor(f"ptI{i}", [4, P], BF16)) for i in range(2)]
                rden = [s3.enter_context(nc.sbuf_tensor(f"rden{i}", [1, P], F32)) for i in range(2)]
                denB = [s3.enter_context(nc.sbuf_tensor(f"denB{i}", [P, P], F32)) for i in range(2)]
                ao_st = [s3.enter_context(nc.sbuf_tensor(f"ao_st{i}", [P, 512], BF16)) for i in range(2)]
                dummy = s3.enter_context(nc.sbuf_tensor("dmy_sb", [P, 512], F32))
                dmy = s3.enter_context(nc.sbuf_tensor("dmy_pt", [P, 2, P], BF16))
                dmy_wq = s3.enter_context(nc.sbuf_tensor("dmy_wq", [P, FC, 256], BF16))
                dmyI = s3.enter_context(nc.sbuf_tensor("dmy_ptI", [4, P], BF16))

                @block.vector
                def _(v):
                    v.memset(pt_sb[0][:, :, :], 0.0)
                    v.memset(pt_sb[1][:, :, :], 0.0).then_inc(sDV.h, 1)
                sDV.n += 1

                for bi in range(KBLK):
                    b = snap()

                    @block.gpsimd
                    def _(g, bi=bi, b=b):
                        for ocg in range(8):
                            if ocg >= 2:
                                g.wait_ge(sPE.h, b[sPE] + 2 * (ocg - 1))
                            else:
                                g.wait_ge(sPE.h, b[sPE])
                            src = wq_t[:, ocg * 256:(ocg + 1) * 256].rearrange("(c p) n -> p c n", p=P)
                            if KSKIP == "qdma":
                                qdma(g, sQD, dmy_wq[:, :, :], src)
                            else:
                                qdma(g, sQD, wqg[ocg % 2][:, :, :], src)

                    @block.tensor
                    def _(te, bi=bi, b=b):
                        te.wait_ge(sDV.h, b[sDV])
                        bc0 = 64 + bi * 512
                        for ocg in range(8):
                            te.wait_ge(sQD.h, b[sQD] + 16 * (ocg + 1))
                            for oi in range(2):
                                oc = ocg * 2 + oi
                                par = oc % 2
                                if oc >= 2:
                                    te.wait_ge(sDV.h, b[sDV] + oc - 1)
                                if KSKIP == "qmm":
                                    mm = te.matmul(PB[par][:, :], dmy_wq[:, 0, 0:P],
                                                   xn1[:, 0, bc0:bc0 + 512], start=True, stop=True)
                                else:
                                    for fc in range(FC):
                                        mm = te.matmul(PB[par][:, :], wqg[ocg % 2][:, fc, oi * P:(oi + 1) * P],
                                                       xn1[:, fc, bc0:bc0 + 512],
                                                       start=(fc == 0), stop=(fc == FC - 1))
                                mm.then_inc(sPE.h, 1)
                    sPE.n += 16

                    @block.vector
                    def _(v, bi=bi, b=b):
                        for oc in range(16):
                            v.wait_ge(sPE.h, b[sPE] + oc + 1)
                            _qdst = dummy[:, :].bitcast(BF16)[:, 0:512] if KSKIP == "qevac" else q_blk[:, oc, :]
                            v.tensor_scalar(_qdst, PB[oc % 2][:, :], QSCALE, None,
                                            op0=ALU.mult).then_inc(sDV.h, 1)
                    sDV.n += 16

                    if "b" not in KSEC:
                        continue
                    b = snap()
                    NU = 64
                    pe_scores = [0] * NU
                    pe_den = [0] * NU
                    pe_av = [0] * NU
                    pe_denB = [0] * NU
                    c = b[sPE]
                    for u in range(NU + 1):
                        if u < NU:
                            c += 1
                            pe_scores[u] = c
                        if u >= 1:
                            c += 1
                            pe_den[u - 1] = c
                            c += 1
                            pe_av[u - 1] = c
                            c += 1
                            pe_denB[u - 1] = c
                    pe_total = c
                    dv_mask = [0] * NU
                    dv_rden = [0] * NU
                    dv_denBe = [0] * NU
                    dv_stt = [0] * NU
                    c = b[sDV]
                    for u in range(NU):
                        c += 1
                        dv_mask[u] = c
                        c += 1
                        dv_rden[u] = c
                        c += 1
                        dv_denBe[u] = c
                        c += 1
                        dv_stt[u] = c
                    dv_total = c
                    ac_exp = [b[sAC] + u + 1 for u in range(NU)]

                    first_mask = (bi == 0)

                    @block.tensor
                    def _(te, bi=bi, b=b, pe_scores=pe_scores, dv_mask=dv_mask,
                          dv_rden=dv_rden, dv_denBe=dv_denBe, dv_stt=dv_stt, ac_exp=ac_exp):
                        te.wait_ge(sDV.h, b[sDV])
                        for u in range(NU + 1):
                            if u < NU:
                                qtl, h = divmod(u, 16)
                                qt = bi * 4 + qtl
                                par = u % 2
                                kvc = h // 2
                                kc0 = qt * P
                                if u >= 2:
                                    te.wait_ge(sAC.h, ac_exp[u - 2])
                                    te.wait_ge(sDV.h, dv_denBe[u - 2])
                                te.matmul(PB[2 + par][:, 0:128], k_sb[:, kvc, kc0:kc0 + P],
                                          q_blk[:, h, qtl * P:(qtl + 1) * P], start=True, stop=True)
                                te.matmul(PB[2 + par][0:64, 128:256], k_sb[:, kvc, kc0 + P:kc0 + P + 64],
                                          q_blk[:, h, qtl * P:(qtl + 1) * P], start=True, stop=True)
                                mm = te.matmul(PB[2 + par][0:4, 256:384], k_sb[:, kvc, ICOL:ICOL + 4],
                                               q_blk[:, h, qtl * P:(qtl + 1) * P], start=True, stop=True)
                                mm.then_inc(sPE.h, 1)
                            if u >= 1:
                                v = u - 1
                                qtl, h = divmod(v, 16)
                                qt = bi * 4 + qtl
                                par = v % 2
                                kvc = h // 2
                                te.wait_ge(sDV.h, dv_mask[v])
                                te.matmul(PB[2 + par][0:1, 384:512], ones_bf[:, :], pt_sb[par][:, 0, :],
                                          start=True, stop=False)
                                te.matmul(PB[2 + par][0:1, 384:512], ones_bf[:, :], pt_sb[par][:, 1, :],
                                          start=False, stop=False)
                                mm = te.matmul(PB[2 + par][0:1, 384:512], ones_bf[0:4, :], ptI_sb[par][0:4, :],
                                               start=False, stop=True)
                                mm.then_inc(sPE.h, 1)
                                if v >= 2:
                                    te.wait_ge(sDV.h, dv_stt[v - 2])
                                te.matmul(PB[4 + par][:, 0:128], vT_sb[:, qt, kvc * P:(kvc + 1) * P],
                                          pt_sb[par][:, 0, :], start=True, stop=False)
                                te.matmul(PB[4 + par][:, 0:128], vT_sb[0:64, qt + 1, kvc * P:(kvc + 1) * P],
                                          pt_sb[par][0:64, 1, :], start=False, stop=False)
                                mm = te.matmul(PB[4 + par][:, 0:128], vI_sb[0:4, kvc * P:(kvc + 1) * P],
                                               ptI_sb[par][0:4, :], start=False, stop=True)
                                mm.then_inc(sPE.h, 1)
                                te.wait_ge(sDV.h, dv_rden[v])
                                mm = te.matmul(PB[4 + par][:, 128:256], ones1_f[:, :], rden[par][:, :],
                                               start=True, stop=True)
                                mm.then_inc(sPE.h, 1)

                    @block.scalar
                    def _(sc, b=b, pe_scores=pe_scores, pe_av=pe_av):
                        for u in range(NU):
                            par = u % 2
                            sc.wait_ge(sPE.h, pe_scores[u])
                            if u >= 2:
                                sc.wait_ge(sPE.h, pe_av[u - 2])
                            if KSKIP == "exp":
                                _d1, _d2, _d3 = (dmy[:, 0, :], dmy[0:64, 1, :], dmyI[0:4, :])
                            else:
                                _d1, _d2, _d3 = (pt_sb[par][:, 0, :], pt_sb[par][0:64, 1, :], ptI_sb[par][0:4, :])
                            sc.activation(_d1, PB[2 + par][:, 0:128], AF.Exp, bias=shift_sb[:, :])
                            sc.activation(_d2, PB[2 + par][0:64, 128:256], AF.Exp, bias=shift_sb[0:64, :])
                            sc.activation(_d3, PB[2 + par][0:4, 256:384], AF.Exp,
                                          bias=shift_sb[0:4, :]).then_inc(sAC.h, 1)
                    sAC.n += NU

                    @block.vector
                    def _(v, bi=bi, b=b, ac_exp=ac_exp, pe_den=pe_den, pe_av=pe_av,
                          pe_denB=pe_denB, first_mask=first_mask):
                        for u in range(NU):
                            qtl, h = divmod(u, 16)
                            qt = bi * 4 + qtl
                            par = u % 2
                            msk = maskF_sb if (first_mask and qtl == 0) else maskM_sb
                            mski = maskFI_sb if (first_mask and qtl == 0) else maskMI_sb
                            v.wait_ge(sAC.h, ac_exp[u])
                            v.tensor_tensor(pt_sb[par][:, :, :], pt_sb[par][:, :, :], msk[:, :, :],
                                            op=ALU.mult)
                            v.tensor_tensor(ptI_sb[par][0:4, :], ptI_sb[par][0:4, :], mski[0:4, :],
                                            op=ALU.mult).then_inc(sDV.h, 1)
                            v.wait_ge(sPE.h, pe_den[u])
                            v.reciprocal(rden[par][:, :], PB[2 + par][0:1, 384:512]).then_inc(sDV.h, 1)
                            v.wait_ge(sPE.h, pe_denB[u])
                            _dbedst = dummy[:, 0:128] if KSKIP == "denbe" else denB[par][:, :]
                            v.tensor_copy(_dbedst, PB[4 + par][:, 128:256]).then_inc(sDV.h, 1)
                            v.wait_ge(sPE.h, pe_av[u])
                            _sttdst = dummy[:, 0:128].bitcast(BF16)[:, 0:128] if KSKIP == "stt" else av_blk[:, h, qtl * P:(qtl + 1) * P]
                            v.scalar_tensor_tensor(_sttdst, PB[4 + par][:, 0:128],
                                                   1.0, denB[par][:, :], op0=ALU.mult,
                                                   op1=ALU.mult).then_inc(sDV.h, 1)
                    sDV.n += 4 * NU
                    assert sPE.n + (4 * NU) == pe_total
                    sPE.n = pe_total
                    assert sDV.n == dv_total

                    if "c" not in KSEC:
                        continue
                    b = snap()

                    @block.gpsimd
                    def _(g, bi=bi, b=b):
                        for ocg in range(8):
                            if ocg >= 2:
                                g.wait_ge(sPE.h, b[sPE] + 2 * (ocg - 1))
                            else:
                                g.wait_ge(sPE.h, b[sPE] - 4 * 64)
                            src = wo_t[:, ocg * 256:(ocg + 1) * 256].rearrange("(c p) n -> p c n", p=P)
                            qdma(g, sQD, wqg[ocg % 2][:, :, :], src)

                    @block.tensor
                    def _(te, bi=bi, b=b):
                        te.wait_ge(sDV.h, b[sDV])
                        for ocg in range(8):
                            te.wait_ge(sQD.h, b[sQD] + 16 * (ocg + 1))
                            for oi in range(2):
                                oc = ocg * 2 + oi
                                par = oc % 2
                                if oc >= 2:
                                    te.wait_ge(sDV.h, b[sDV] + oc - 1)
                                for fc in range(FC):
                                    mm = te.matmul(PB[par][:, :], wqg[ocg % 2][:, fc, oi * P:(oi + 1) * P],
                                                   av_blk[:, fc, :],
                                                   start=(fc == 0), stop=(fc == FC - 1))
                                mm.then_inc(sPE.h, 1)
                    sPE.n += 16

                    @block.vector
                    def _(v, bi=bi, b=b):
                        for oc in range(16):
                            v.wait_ge(sPE.h, b[sPE] + oc + 1)
                            if oc >= 2:
                                v.wait_ge(sSD.h, b[sSD] + 16 * (oc - 1))
                            v.tensor_copy(ao_st[oc % 2][:, :], PB[oc % 2][:, :]).then_inc(sDV.h, 1)
                    sDV.n += 16

                    @block.scalar
                    def _(sc, bi=bi, b=b):
                        for oc in range(16):
                            sc.wait_ge(sDV.h, b[sDV] + oc + 1)
                            dst = ao_scr[oc * P:(oc + 1) * P, bi * 512:(bi + 1) * 512]
                            qdma(sc, sSD, dst, ao_st[oc % 2][:, :])

        s13.close()

        s58 = st.enter_context(contextlib.ExitStack())
        xn2 = s58.enter_context(nc.sbuf_tensor("xn2", [P, FC, T], BF16))
        w8t_sb = s58.enter_context(nc.sbuf_tensor("w8t_sb", [8, T], F32))
        s45 = st.enter_context(contextlib.ExitStack())
        R2m = s45.enter_context(nc.sbuf_tensor("R2m", [1, T], F32, side="right"))
        R2r = s45.enter_context(nc.sbuf_tensor("R2r", [1, T], F32, side="right"))
        R2mr = s45.enter_context(nc.sbuf_tensor("R2mr", [1, T], F32, side="right"))
        if PHASES >= 4:
            s4 = contextlib.ExitStack()
            with s4:
                aob = s4.enter_context(nc.sbuf_tensor("aob", [P, FC, 512], BF16))
                x1b = s4.enter_context(nc.sbuf_tensor("x1b", [P, FC, 512], F32))
                x1bf = s4.enter_context(nc.sbuf_tensor("x1bf", [P, FC, 512], BF16))
                Rq4 = s4.enter_context(nc.sbuf_tensor("Rq4", [1, 2, 512], F32, side="right"))
                Rmsq4 = s4.enter_context(nc.sbuf_tensor("Rmsq4", [1, 2, 512], F32, side="right"))
                Rsd4 = s4.enter_context(nc.sbuf_tensor("Rsd4", [1, 2, 512], F32, side="right"))
                b = snap()

                @block.sync
                def _(s):
                    wait_all(s, [sSD], b)
                    for i in range(4):
                        if i >= 1:
                            s.wait_ge(sSD.h, b[sSD] + 16 * i)
                            s.wait_ge(sAC.h, b[sAC] + 2 * (i - 1) + 1)
                        c0 = 64 + i * 512
                        src = x_slab[:, c0:c0 + 512].rearrange("(c p) t -> p c t", p=P)
                        qdma(s, sHD, x1b[:, :, :], src)

                @block.gpsimd
                def _(g):
                    wait_all(g, [sSD], b)
                    for i in range(4):
                        if i >= 1:
                            g.wait_ge(sDV.h, b[sDV] + 4 * (i - 1) + 1)
                        src = ao_scr[:, i * 512:(i + 1) * 512].rearrange("(c p) t -> p c t", p=P)
                        qdma(g, sQD, aob[:, :, :], src)

                @block.vector
                def _(v):
                    for i in range(4):
                        ip = i % 2
                        v.wait_ge(sHD.h, b[sHD] + 16 * (i + 1))
                        v.wait_ge(sQD.h, b[sQD] + 16 * (i + 1))
                        for fc in range(FC):
                            tt = v.tensor_tensor(x1b[:, fc, :], x1b[:, fc, :], aob[:, fc, :], op=ALU.add)
                        tt.then_inc(sDV.h, 1)
                        if i >= 1:
                            v.wait_ge(sPE.h, b[sPE] + 2 * i)
                        v.tensor_copy(x1bf[:, :, :], x1b[:, :, :]).then_inc(sDV.h, 1)
                        v.wait_ge(sPE.h, b[sPE] + 2 * i + 2)
                        if i >= 2:
                            v.wait_ge(sAC.h, b[sAC] + 2 * (i - 2) + 2)
                        v.tensor_scalar(R2m[:, i * 512:(i + 1) * 512], PB[0][0:1, :], INVD, None, op0=ALU.mult)
                        v.tensor_scalar(Rq4[:, ip, :], PB[1][0:1, :], INVD, None, op0=ALU.mult)
                        v.tensor_tensor(Rmsq4[:, ip, :], R2m[:, i * 512:(i + 1) * 512],
                                        R2m[:, i * 512:(i + 1) * 512], op=ALU.mult)
                        v.tensor_tensor(Rq4[:, ip, :], Rq4[:, ip, :], Rmsq4[:, ip, :],
                                        op=ALU.subtract)
                        v.tensor_scalar(Rq4[:, ip, :], Rq4[:, ip, :], EPS, None,
                                        op0=ALU.add).then_inc(sDV.h, 1)
                        v.wait_ge(sAC.h, b[sAC] + 2 * i + 2)
                        v.reciprocal(R2r[:, i * 512:(i + 1) * 512], Rsd4[:, ip, :])
                        v.tensor_tensor(R2mr[:, i * 512:(i + 1) * 512], R2m[:, i * 512:(i + 1) * 512],
                                        R2r[:, i * 512:(i + 1) * 512], op=ALU.mult).then_inc(sDV.h, 1)
                sDV.n += 4 * 4

                @block.scalar
                def _(sc):
                    for i in range(4):
                        ip = i % 2
                        sc.wait_ge(sPE.h, b[sPE] + 2 * i + 1)
                        sc.activation(x1bf[:, :, :], x1b[:, :, :], AF.Square).then_inc(sAC.h, 1)
                        sc.wait_ge(sDV.h, b[sDV] + 4 * i + 3)
                        sc.activation(Rsd4[:, ip, :], Rq4[:, ip, :], AF.Sqrt).then_inc(sAC.h, 1)
                        qdma(sc, sSD, out[:, i * 512:(i + 1) * 512].rearrange("(c p) t -> p c t", p=P),
                                     x1b[:, :, :])
                sAC.n += 2 * 4

                @block.tensor
                def _(te):
                    for i in range(4):
                        te.wait_ge(sDV.h, b[sDV] + 4 * i + 2)
                        if i >= 1:
                            te.wait_ge(sDV.h, b[sDV] + 4 * (i - 1) + 3)
                        for fc in range(FC):
                            mm = te.matmul(PB[0][0:1, :], ones_bf[:, :], x1bf[:, fc, :],
                                           start=(fc == 0), stop=(fc == FC - 1))
                        mm.then_inc(sPE.h, 1)
                        te.wait_ge(sAC.h, b[sAC] + 2 * i + 1)
                        for fc in range(FC):
                            mm = te.matmul(PB[1][0:1, :], ones_bf[:, :], x1bf[:, fc, :],
                                           start=(fc == 0), stop=(fc == FC - 1))
                        mm.then_inc(sPE.h, 1)
                sPE.n += 2 * 4

        if PHASES >= 5:
            s5 = contextlib.ExitStack()
            with s5:
                x1in = s5.enter_context(nc.sbuf_tensor("x1in", [P, FC, 512], F32))
                xnf = [s5.enter_context(nc.sbuf_tensor(f"xnf{i}", [P, FC, 512], F32)) for i in range(2)]
                t15 = s5.enter_context(nc.sbuf_tensor("t15", [P, 512], F32))
                l_sb = s5.enter_context(nc.sbuf_tensor("l_sb", [P, 128], F32))
                w8_sb = s5.enter_context(nc.sbuf_tensor("w8_sb", [P, 128], F32))
                e8 = s5.enter_context(nc.sbuf_tensor("e8", [P, 8], F32))
                p8 = s5.enter_context(nc.sbuf_tensor("p8", [P, 8], F32))
                sc8 = s5.enter_context(nc.sbuf_tensor("sc8", [P, 6], F32))
                rdbg = s5.enter_context(nc.sbuf_tensor("rdbg", [P, 32], F32))
                b = snap()

                @block.sync
                def _(s):
                    wait_all(s, [sSD], b)
                    for cc in range(4):
                        if cc >= 1:
                            s.wait_ge(sDV.h, b[sDV] + 42 * (cc - 1) + 1)
                        qdma(s, sHD, x1in[:, :, :],
                                    out[:, cc * 512:(cc + 1) * 512].rearrange("(c p) t -> p c t", p=P)
                                    )

                @block.tensor
                def _(te):
                    for cc in range(4):
                        par = cc % 2
                        te.wait_ge(sDV.h, b[sDV])
                        if cc >= 2:
                            te.wait_ge(sDV.h, b[sDV] + 42 * (cc - 2) + 1)
                        te.matmul(PB[0 + par][:, :], ones1_f[:, :], R2r[:, cc * 512:(cc + 1) * 512],
                                  start=True, stop=True)
                        mm = te.matmul(PB[2 + par][:, :], ones1_f[:, :], R2mr[:, cc * 512:(cc + 1) * 512],
                                       start=True, stop=True)
                        mm.then_inc(sPE.h, 1)
                        te.wait_ge(sDV.h, b[sDV] + 42 * cc + 1)
                        if cc >= 2:
                            te.wait_ge(sDV.h, b[sDV] + 42 * (cc - 2) + 2)
                        for j in range(4):
                            for fc in range(FC):
                                mm = te.matmul(PB[4 + par][:, j * 8:(j + 1) * 8],
                                               xnf[par][:, fc, j * P:(j + 1) * P],
                                               gate_sb[:, fc, :],
                                               start=(fc == 0), stop=(fc == FC - 1))
                        mm.then_inc(sPE.h, 1)
                sPE.n += 2 * 4

                @block.vector
                def _(v):
                    dvc = b[sDV]
                    for cc in range(4):
                        par = cc % 2
                        v.wait_ge(sHD.h, b[sHD] + 16 * (cc + 1))
                        v.wait_ge(sPE.h, b[sPE] + 2 * cc + 1)
                        for fc in range(FC):
                            v.tensor_tensor(t15[:, :], x1in[:, fc, :], PB[0 + par][:, :], op=ALU.mult)
                            tt = v.tensor_tensor(xnf[par][:, fc, :], t15[:, :], PB[2 + par][:, :],
                                                 op=ALU.subtract)
                        tt.then_inc(sDV.h, 1)
                        dvc += 1
                        v.tensor_copy(xn2[:, :, cc * 512:(cc + 1) * 512], xnf[par][:, :, :])
                        v.wait_ge(sPE.h, b[sPE] + 2 * cc + 2)
                        v.tensor_copy(l_sb[:, cc * 32:(cc + 1) * 32], PB[4 + par][:, 0:32]).then_inc(sDV.h, 1)
                        dvc += 1
                        for j in range(cc * 4, cc * 4 + 4):
                            lj = l_sb[:, j * 8:(j + 1) * 8]

                            def _st(instr):
                                nonlocal dvc
                                instr.then_inc(sDV.h, 1)
                                dvc += 1
                                v.wait_ge(sDV.h, dvc)
                            _st(v.reduce_max(sc8[:, 0:1], lj, axis=AX.X))
                            _st(v.reduce_max(sc8[:, 1:2], lj, axis=AX.X, negate=True))
                            _st(v.tensor_tensor(e8[:, :], lj, sc8[:, 0:1].broadcast_to([P, 8]),
                                                op=ALU.is_ge))
                            _st(v.scalar_tensor_tensor(p8[:, :], e8[:, :], -60.0, lj,
                                                       op0=ALU.mult, op1=ALU.add))
                            _st(v.reduce_max(sc8[:, 2:3], p8[:, :], axis=AX.X))
                            _st(v.tensor_tensor(e8[:, :], lj, sc8[:, 2:3].broadcast_to([P, 8]),
                                                op=ALU.is_ge))
                            v.wait_ge(sAC.h, b[sAC] + j + 1)
                            _st(v.reduce_sum(sc8[:, 3:4], p8[:, :], axis=AX.X))
                            _st(v.reciprocal(sc8[:, 4:5], sc8[:, 3:4]))
                            _st(v.tensor_tensor(p8[:, :], p8[:, :], sc8[:, 4:5].broadcast_to([P, 8]),
                                                op=ALU.mult))
                            _st(v.tensor_tensor(w8_sb[:, j * 8:(j + 1) * 8], p8[:, :], e8[:, :],
                                                op=ALU.mult))
                sDV.n += 4 * 42

                @block.scalar
                def _(sc):
                    for cc in range(4):
                        for j in range(cc * 4, cc * 4 + 4):
                            sc.wait_ge(sDV.h, b[sDV] + 42 * cc + 2 + (j - cc * 4) * 10 + 5)
                            sc.activation(p8[:, :], l_sb[:, j * 8:(j + 1) * 8], AF.Exp,
                                          bias=sc8[:, 1:2]).then_inc(sAC.h, 1)
                sAC.n += 16

                b2 = snap()

                @block.tensor
                def _(te):
                    wait_all(te, [sDV], b2)
                    for j in range(16):
                        if j >= 1:
                            te.wait_ge(sDV.h, b2[sDV] + j)
                        mm = te.transpose(PB[6][0:8, 0:128], w8_sb[:, j * 8:(j + 1) * 8], eye_sb[:, :])
                        mm.then_inc(sPE.h, 1)
                sPE.n += 16

                @block.vector
                def _(v):
                    for j in range(16):
                        v.wait_ge(sPE.h, b2[sPE] + j + 1)
                        v.tensor_copy(w8t_sb[0:8, j * P:(j + 1) * P], PB[6][0:8, 0:128]).then_inc(sDV.h, 1)
                sDV.n += 16

        s45.close()
        s67 = st.enter_context(contextlib.ExitStack())
        sigB_sb = s67.enter_context(nc.sbuf_tensor("sigB_sb", [P, T], F32, side="right"))

        if PHASES >= 6:
            s6 = contextlib.ExitStack()
            with s6:
                sig_row = s6.enter_context(nc.sbuf_tensor("sig_row", [1, T], F32))
                gub = [s6.enter_context(nc.sbuf_tensor(f"gub{i}", [P, FC, 256], BF16)) for i in range(2)]
                hs_st = [s6.enter_context(nc.sbuf_tensor(f"hs_st{i}", [P, T], BF16)) for i in range(2)]
                s_act = [s6.enter_context(nc.sbuf_tensor(f"s_act{i}", [P, 512], F32)) for i in range(2)]
                b = snap()

                if "sg" in KP6:
                    @block.tensor
                    def _(te):
                        wait_all(te, [sDV], b)
                        for cc in range(4):
                            par = cc % 2
                            if cc >= 2:
                                te.wait_ge(sAC.h, b[sAC] + cc - 1)
                            for fc in range(FC):
                                mm = te.matmul(PB[4 + par][0:1, :], sgw_sb[:, fc, :],
                                               xn2[:, fc, cc * 512:(cc + 1) * 512],
                                               start=(fc == 0), stop=(fc == FC - 1))
                            mm.then_inc(sPE.h, 1)
                        for cc in range(4):
                            te.wait_ge(sAC.h, b[sAC] + cc + 1)
                            if cc >= 1:
                                te.wait_ge(sDV.h, b[sDV] + cc)
                            mm = te.matmul(PB[6][:, :], ones1_f[:, :], sig_row[:, cc * 512:(cc + 1) * 512],
                                           start=True, stop=True)
                            mm.then_inc(sPE.h, 1)
                    sPE.n += 8

                    @block.scalar
                    def _(sc):
                        for cc in range(4):
                            par = cc % 2
                            sc.wait_ge(sPE.h, b[sPE] + cc + 1)
                            sc.activation(sig_row[:, cc * 512:(cc + 1) * 512], PB[4 + par][0:1, :],
                                          AF.Sigmoid).then_inc(sAC.h, 1)
                    sAC.n += 4

                    @block.vector
                    def _(v):
                        for cc in range(4):
                            v.wait_ge(sPE.h, b[sPE] + 4 + cc + 1)
                            v.tensor_copy(sigB_sb[:, cc * 512:(cc + 1) * 512], PB[6][:, :]).then_inc(sDV.h, 1)
                    sDV.n += 4

                GU = "gu" in KP6
                b = snap()
                NJ = KNJ

                if GU:
                  @block.gpsimd
                  def _(g):
                    for j in range(NJ):
                        if j >= 2:
                            g.wait_ge(sPE.h, b[sPE] + 4 * (j - 1))
                        else:
                            g.wait_ge(sPE.h, b[sPE])
                        src1 = gu_t[:, j * P:(j + 1) * P].rearrange("(c p) n -> p c n", p=P)
                        src2 = gu_t[:, SH_I + j * P:SH_I + (j + 1) * P].rearrange("(c p) n -> p c n", p=P)
                        qdma(g, sQD, gub[j % 2][:, :, 0:P], src1)
                        qdma(g, sQD, gub[j % 2][:, :, P:256], src2)

                if GU:
                  @block.tensor
                  def _(te):
                    for j in range(NJ):
                        te.wait_ge(sQD.h, b[sQD] + 32 * (j + 1))
                        for tc in range(4):
                            gi = j * 4 + tc
                            par = gi % 2
                            if gi >= 2:
                                te.wait_ge(sAC.h, b[sAC] + gi - 1)
                                te.wait_ge(sDV.h, b[sDV] + gi - 1)
                            for fc in range(FC):
                                te.matmul(PB[0 + par][:, :], gub[j % 2][:, fc, 0:P],
                                          xn2[:, fc, tc * 512:(tc + 1) * 512],
                                          start=(fc == 0), stop=(fc == FC - 1))
                            for fc in range(FC):
                                mm = te.matmul(PB[2 + par][:, :], gub[j % 2][:, fc, P:256],
                                               xn2[:, fc, tc * 512:(tc + 1) * 512],
                                               start=(fc == 0), stop=(fc == FC - 1))
                            mm.then_inc(sPE.h, 1)
                if GU:
                    sPE.n += 4 * NJ

                if GU:
                  @block.scalar
                  def _(sc):
                    for j in range(NJ):
                        for tc in range(4):
                            gi = j * 4 + tc
                            par = gi % 2
                            sc.wait_ge(sPE.h, b[sPE] + gi + 1)
                            if gi >= 2:
                                sc.wait_ge(sDV.h, b[sDV] + gi - 1)
                            sc.activation(s_act[par][:, :], PB[0 + par][:, :],
                                          AF.Silu).then_inc(sAC.h, 1)
                        sc.wait_ge(sDV.h, b[sDV] + 4 * (j + 1))
                        qdma(sc, sSD, hs_scr[j * P:(j + 1) * P, :], hs_st[j % 2][:, :])
                if GU:
                    sAC.n += 4 * NJ

                if GU:
                  @block.vector
                  def _(v):
                    for j in range(NJ):
                        for tc in range(4):
                            gi = j * 4 + tc
                            par = gi % 2
                            v.wait_ge(sAC.h, b[sAC] + gi + 1)
                            if j >= 2 and tc == 0:
                                v.wait_ge(sSD.h, b[sSD] + 16 * (j - 1))
                            v.tensor_tensor(hs_st[j % 2][:, tc * 512:(tc + 1) * 512], PB[2 + par][:, :],
                                            s_act[par][:, :], op=ALU.mult).then_inc(sDV.h, 1)
                if GU:
                    sDV.n += 4 * NJ



        if PHASES >= 7:
            s7 = contextlib.ExitStack()
            with s7:
                hs_sl = s7.enter_context(nc.sbuf_tensor("hs_sl", [P, 44, 1024], BF16))
                dnb = [s7.enter_context(nc.sbuf_tensor(f"dnb{i}", [P, 44, P], BF16)) for i in range(2)]
                o_t = [s7.enter_context(nc.sbuf_tensor(f"o_t{i}", [P, 512], F32)) for i in range(4)]
                tmp7 = [s7.enter_context(nc.sbuf_tensor(f"tmp7{i}", [P, 512], F32)) for i in range(2)]
                for hf in range(2):
                    b = snap()

                    @block.gpsimd
                    def _(g, hf=hf, b=b):
                        wait_all(g, [sSD, sPE], b)
                        for piece in range(4):
                            src = hs_scr[piece * 1408:(piece + 1) * 1408, hf * 1024:(hf + 1) * 1024]
                            src = src.rearrange("(c p) t -> p c t", p=P)
                            qdma(g, sQD, hs_sl[:, piece * 11:(piece + 1) * 11, :], src)
                        for oc in range(16):
                            if oc >= 2:
                                g.wait_ge(sPE.h, b[sPE] + 2 * (oc - 1))
                            src = dn_t[:, oc * P:(oc + 1) * P].rearrange("(c p) n -> p c n", p=P)
                            qdma(g, sQD, dnb[oc % 2][:, :, :], src)

                    @block.tensor
                    def _(te, hf=hf, b=b):
                        te.wait_ge(sQD.h, b[sQD] + 16 * 4)
                        for oc in range(16):
                            te.wait_ge(sQD.h, b[sQD] + 16 * (4 + oc + 1))
                            for tc2 in range(2):
                                gi = oc * 2 + tc2
                                par = gi % 4
                                if gi >= 4:
                                    te.wait_ge(sDV.h, b[sDV] + 2 * (gi - 3) - 1)
                                for mc in range(44):
                                    mm = te.matmul(PB[par][:, :], dnb[oc % 2][:, mc, :],
                                                   hs_sl[:, mc, tc2 * 512:(tc2 + 1) * 512],
                                                   start=(mc == 0), stop=(mc == 43))
                                mm.then_inc(sPE.h, 1)
                    sPE.n += 32

                    @block.sync
                    def _(s, hf=hf, b=b):
                        wait_all(s, [sSD], b)
                        for gi in range(32):
                            oc, tc2 = divmod(gi, 2)
                            if gi >= 4:
                                s.wait_ge(sSD.h, b[sSD] + 16 * (gi - 3))
                            c0 = hf * 1024 + tc2 * 512
                            qdma(s, sHD, o_t[gi % 4][:, :], out[oc * P:(oc + 1) * P, c0:c0 + 512]
                                        )

                    @block.vector
                    def _(v, hf=hf, b=b):
                        for gi in range(32):
                            oc, tc2 = divmod(gi, 2)
                            par = gi % 4
                            c0 = hf * 1024 + tc2 * 512
                            v.wait_ge(sPE.h, b[sPE] + gi + 1)
                            v.tensor_tensor(tmp7[gi % 2][:, :], PB[par][:, :], sigB_sb[:, c0:c0 + 512],
                                            op=ALU.mult).then_inc(sDV.h, 1)
                            v.wait_ge(sHD.h, b[sHD] + 16 * (gi + 1))
                            v.tensor_tensor(o_t[gi % 4][:, :], o_t[gi % 4][:, :], tmp7[gi % 2][:, :],
                                            op=ALU.add).then_inc(sDV.h, 1)
                    sDV.n += 64

                    @block.scalar
                    def _(sc, hf=hf, b=b):
                        for gi in range(32):
                            oc, tc2 = divmod(gi, 2)
                            c0 = hf * 1024 + tc2 * 512
                            sc.wait_ge(sDV.h, b[sDV] + 2 * gi + 2)
                            qdma(sc, sSD, out[oc * P:(oc + 1) * P, c0:c0 + 512], o_t[gi % 4][:, :]
                                         )

        s67.close()

        if PHASES >= 8:
            s8 = contextlib.ExitStack()
            with s8:
                comb = s8.enter_context(nc.sbuf_tensor("comb", [P, FC, 1024], F32))
                mid = s8.enter_context(nc.sbuf_tensor("mid", [P, 11, 1024], BF16))
                w13b = [s8.enter_context(nc.sbuf_tensor(f"w13b{i}", [P, FC, 256], BF16)) for i in range(2)]
                w2b = [s8.enter_context(nc.sbuf_tensor(f"w2b{i}", [P, 11, P], BF16)) for i in range(2)]
                wB = s8.enter_context(nc.sbuf_tensor("wB", [P, 1024], F32))
                wrow = s8.enter_context(nc.sbuf_tensor("wrow", [1, 1024], F32))
                sact8 = [s8.enter_context(nc.sbuf_tensor(f"sact8{i}", [P, 512], F32, side="right")) for i in range(2)]
                uw8 = [s8.enter_context(nc.sbuf_tensor(f"uw8{i}", [P, 512], F32, side="right")) for i in range(2)]
                o_t8 = [s8.enter_context(nc.sbuf_tensor(f"o_t8{i}", [P, 512], F32)) for i in range(4)]
                for hf in range(2):
                    bh = snap()

                    @block.vector
                    def _(v, hf=hf, bh=bh):
                        wait_all(v, [sSD, sPE], bh)
                        v.memset(comb[:, :, :], 0.0).then_inc(sDV.h, 1)
                    sDV.n += 1

                    for e in range(NEXP):
                        b = snap()
                        @block.gpsimd
                        def _(g, e=e, hf=hf, b=b):
                            if PHASES >= 5:
                                g.wait_ge(sDV.h, b[sDV])
                            qdma(g, sQD, wrow[:, 0:1024], w8t_sb[e:e + 1, hf * 1024:(hf + 1) * 1024]
                                        )

                        @block.tensor
                        def _(te, e=e, b=b):
                            te.wait_ge(sQD.h, b[sQD] + 16)
                            if e >= 1 or True:
                                te.wait_ge(sDV.h, b[sDV])
                            te.matmul(PB[6][:, :], ones1_f[:, :], wrow[:, 0:512], start=True, stop=True)
                            mm = te.matmul(PB[7][:, :], ones1_f[:, :], wrow[:, 512:1024], start=True, stop=True)
                            mm.then_inc(sPE.h, 1)
                        sPE.n += 1

                        @block.vector
                        def _(v, e=e, b=b):
                            v.wait_ge(sPE.h, b[sPE] + 1)
                            v.tensor_copy(wB[:, 0:512], PB[6][:, :])
                            v.tensor_copy(wB[:, 512:1024], PB[7][:, :]).then_inc(sDV.h, 1)
                        sDV.n += 1

                        b = snap()

                        @block.gpsimd
                        def _(g, e=e, b=b):
                            for mc in range(11):
                                if mc >= 2:
                                    g.wait_ge(sPE.h, b[sPE] + 2 * (mc - 1))
                                src1 = w13_t[e, :, mc * P:(mc + 1) * P].rearrange("(c p) n -> p c n", p=P)
                                src2 = w13_t[e, :, MOE_I + mc * P:MOE_I + (mc + 1) * P].rearrange(
                                    "(c p) n -> p c n", p=P)
                                qdma(g, sQD, w13b[mc % 2][:, :, 0:P], src1)
                                qdma(g, sQD, w13b[mc % 2][:, :, P:256], src2)

                        @block.tensor
                        def _(te, e=e, hf=hf, b=b):
                            for mc in range(11):
                                te.wait_ge(sQD.h, b[sQD] + 32 * (mc + 1))
                                for tc in range(2):
                                    gi = mc * 2 + tc
                                    par = gi % 2
                                    if gi >= 2:
                                        te.wait_ge(sAC.h, b[sAC] + gi - 1)
                                        te.wait_ge(sDV.h, b[sDV] + 2 * (gi - 1))
                                    c0 = hf * 1024 + tc * 512
                                    for fc in range(FC):
                                        te.matmul(PB[0 + par][:, :], w13b[mc % 2][:, fc, 0:P],
                                                  xn2[:, fc, c0:c0 + 512],
                                                  start=(fc == 0), stop=(fc == FC - 1))
                                    for fc in range(FC):
                                        mm = te.matmul(PB[2 + par][:, :], w13b[mc % 2][:, fc, P:256],
                                                       xn2[:, fc, c0:c0 + 512],
                                                       start=(fc == 0), stop=(fc == FC - 1))
                                    mm.then_inc(sPE.h, 1)
                        sPE.n += 22

                        @block.scalar
                        def _(sc, e=e, b=b):
                            for gi in range(22):
                                par = gi % 2
                                sc.wait_ge(sPE.h, b[sPE] + gi + 1)
                                if gi >= 2:
                                    sc.wait_ge(sDV.h, b[sDV] + 2 * gi - 2)
                                sc.activation(sact8[par][:, :], PB[0 + par][:, :], AF.Silu).then_inc(sAC.h, 1)
                        sAC.n += 22

                        @block.vector
                        def _(v, e=e, hf=hf, b=b):
                            for gi in range(22):
                                mc, tc = divmod(gi, 2)
                                par = gi % 2
                                v.wait_ge(sAC.h, b[sAC] + gi + 1)
                                v.tensor_tensor(uw8[par][:, :], PB[2 + par][:, :],
                                                wB[:, tc * 512:(tc + 1) * 512], op=ALU.mult)
                                v.tensor_tensor(mid[:, mc, tc * 512:(tc + 1) * 512], uw8[par][:, :],
                                                sact8[par][:, :], op=ALU.mult).then_inc(sDV.h, 2)
                        sDV.n += 44

                        b = snap()

                        @block.gpsimd
                        def _(g, e=e, b=b):
                            for oc in range(16):
                                if oc >= 2:
                                    g.wait_ge(sPE.h, b[sPE] + 2 * (oc - 1))
                                src = w2_t[e, :, oc * P:(oc + 1) * P].rearrange("(c p) n -> p c n", p=P)
                                qdma(g, sQD, w2b[oc % 2][:, :, :], src)

                        @block.tensor
                        def _(te, e=e, b=b):
                            te.wait_ge(sDV.h, b[sDV])
                            for oc in range(16):
                                te.wait_ge(sQD.h, b[sQD] + 16 * (oc + 1))
                                for tc in range(2):
                                    gi = oc * 2 + tc
                                    par = 4 + gi % 2
                                    if gi >= 2:
                                        te.wait_ge(sDV.h, b[sDV] + gi - 1)
                                    for mc in range(11):
                                        mm = te.matmul(PB[par][:, :], w2b[oc % 2][:, mc, :],
                                                       mid[:, mc, tc * 512:(tc + 1) * 512],
                                                       start=(mc == 0), stop=(mc == 10))
                                    mm.then_inc(sPE.h, 1)
                        sPE.n += 32

                        @block.vector
                        def _(v, e=e, b=b):
                            for gi in range(32):
                                oc, tc = divmod(gi, 2)
                                par = 4 + gi % 2
                                v.wait_ge(sPE.h, b[sPE] + gi + 1)
                                v.tensor_tensor(comb[:, oc, tc * 512:(tc + 1) * 512],
                                                comb[:, oc, tc * 512:(tc + 1) * 512],
                                                PB[par][:, :], op=ALU.add).then_inc(sDV.h, 1)
                        sDV.n += 32

                    b = snap()

                    @block.sync
                    def _(s, hf=hf, b=b):
                        wait_all(s, [sSD, sDV], b)
                        for gi in range(32):
                            oc, tc = divmod(gi, 2)
                            if gi >= 4:
                                s.wait_ge(sSD.h, b[sSD] + 16 * (gi - 3))
                            c0 = hf * 1024 + tc * 512
                            qdma(s, sHD, o_t8[gi % 4][:, :], out[oc * P:(oc + 1) * P, c0:c0 + 512]
                                        )

                    @block.vector
                    def _(v, hf=hf, b=b):
                        for gi in range(32):
                            oc, tc = divmod(gi, 2)
                            v.wait_ge(sHD.h, b[sHD] + 16 * (gi + 1))
                            v.tensor_tensor(o_t8[gi % 4][:, :], o_t8[gi % 4][:, :],
                                            comb[:, oc, tc * 512:(tc + 1) * 512], op=ALU.add).then_inc(sDV.h, 1)
                    sDV.n += 32

                    @block.scalar
                    def _(sc, hf=hf, b=b):
                        for gi in range(32):
                            oc, tc = divmod(gi, 2)
                            c0 = hf * 1024 + tc * 512
                            sc.wait_ge(sDV.h, b[sDV] + gi + 1)
                            qdma(sc, sSD, out[oc * P:(oc + 1) * P, c0:c0 + 512], o_t8[gi % 4][:, :]
                                         )

        if KDBG == "k":
            fin0 = snap()

            @block.gpsimd
            def _(g):
                wait_all(g, ALL, fin0)
                qdma(g, sQD, out[0:1024, :].rearrange("(c p) t -> p c t", p=P),
                     k_sb[:, :, 64:64 + T])

        if KDBG == "v":
            fin0 = snap()

            @block.gpsimd
            def _(g):
                wait_all(g, ALL, fin0)
                qdma(g, sQD, dbg.rearrange("(tt p) o -> p tt o", p=P), vT_sb[:, :, :])

        if KDBG == "rt":
            fin0 = snap()

            @block.gpsimd
            def _(g):
                wait_all(g, ALL, fin0)
                qdma(g, sQD, dbg[128:256, 0:32], rdbg[:, :])

        if KDBG == "w8":
            fin0 = snap()

            @block.gpsimd
            def _(g):
                wait_all(g, ALL, fin0)
                qdma(g, sQD, dbg[0:8, 0:1024], w8t_sb[0:8, 0:1024])
                qdma(g, sQD, dbg[8:16, 0:1024], w8t_sb[0:8, 1024:2048])

        if KDBG == "xn1":
            fin0 = snap()

            @block.gpsimd
            def _(g):
                wait_all(g, ALL, fin0)
                qdma(g, sQD, out.rearrange("(c p) t -> p c t", p=P),
                            xn1[:, :, 64:64 + T])
        if KDBG == "ao":
            fin0 = snap()

            @block.gpsimd
            def _(g):
                wait_all(g, ALL, fin0)
                qdma(g, sQD, out, ao_scr)

        fin = snap()

        @block.sync
        def _(s):
            s.wait_ge(sSD.h, fin[sSD])
            s.wait_ge(sQD.h, fin[sQD])

        @block.gpsimd
        def _(g):
            g.wait_ge(sSD.h, fin[sSD])

    return nc


def _masks():
    r = np.arange(P)[None, :]
    cb = np.arange(P)[:, None]
    mm = np.zeros((P, 2, P), np.float32)
    mm[:, 0, :] = ((cb >= r + 57) & (cb <= r + 64)).astype(np.float32)
    cbB = (128 + np.arange(P))[:, None]
    bm = ((cbB >= r + 57) & (cbB <= r + 64)).astype(np.float32)
    mm[0:64, 1, :] = bm[0:64]
    mmi = np.ones((4, P), np.float32)

    mf = np.zeros((P, 2, P), np.float32)
    j = cb - 64
    mf[:, 0, :] = ((j >= 4) & (j <= r) & (r - j < 8)).astype(np.float32)
    jB = cbB - 64
    bf = ((jB >= 4) & (jB <= r) & (r - jB < 8)).astype(np.float32)
    mf[0:64, 1, :] = bf[0:64]
    jj = np.arange(4)[:, None]
    mfi = (jj <= r).astype(np.float32)
    return (np.ascontiguousarray(mf.reshape(P, 256)),
            np.ascontiguousarray(mm.reshape(P, 256)),
            np.ascontiguousarray(mfi), np.ascontiguousarray(mmi))


def kernel(L0_attn_x, L0_attn_Wq, L0_attn_Wk, L0_attn_Wv, L0_attn_Wo,
           L0_ffn_gate_w, L0_ffn_shared_gate_w, L0_ffn_shared_gate_up_w,
           L0_ffn_shared_down_w, L0_ffn_w13, L0_ffn_w2):
    x = np.asarray(L0_attn_x, dtype=np.float32)
    cw = lambda a: np.ascontiguousarray(np.asarray(a, dtype=np.float32))
    wq_t = cw(np.asarray(L0_attn_Wq).T)
    wk_t = cw(np.asarray(L0_attn_Wk).T)
    wv_t = cw(np.asarray(L0_attn_Wv).T)
    wo_t = cw(np.asarray(L0_attn_Wo).T)
    gate_t = cw(np.asarray(L0_ffn_gate_w).T)
    sg_t = cw(np.asarray(L0_ffn_shared_gate_w).T)
    gu_t = cw(np.asarray(L0_ffn_shared_gate_up_w).T)
    dn_t = cw(np.asarray(L0_ffn_shared_down_w).T)
    w13 = np.asarray(L0_ffn_w13, dtype=np.float32)
    w2 = np.asarray(L0_ffn_w2, dtype=np.float32)
    w13_t = cw(np.transpose(w13, (0, 2, 1)))
    w2_t = cw(np.transpose(w2, (0, 2, 1)))
    gate_pt = cw(gate_t.reshape(FC, P, 8).transpose(1, 0, 2))
    sg_pt = cw(sg_t.reshape(FC, P, 1).transpose(1, 0, 2))
    eye = np.eye(P, dtype=np.float32)
    mF, mM, mFI, mMI = _masks()

    nc = build_nc()

    in_maps = []
    for c in range(8):
        bb, si = divmod(c, 4)
        s0 = si * T
        xf = x[bb].T
        pre = xf[:, s0 - PRE:s0] if s0 > 0 else np.zeros((D, PRE), np.float32)
        slab = np.concatenate([pre, xf[:, s0:s0 + T], xf[:, 0:4],
                               np.zeros((D, W - T - PRE - 4), np.float32)], axis=1)
        in_maps.append(dict(
            x_slab=np.ascontiguousarray(slab), wq_t=wq_t, wk_t=wk_t, wv_t=wv_t, wo_t=wo_t,
            gate_pt=gate_pt, sg_pt=sg_pt, gu_t=gu_t, dn_t=dn_t, w13_t=w13_t, w2_t=w2_t,
            maskF=(mF if s0 == 0 else mM), maskM=mM,
            maskFI=(mFI if s0 == 0 else mMI), maskMI=mMI, eye_in=eye))

    res = run_bass_kernel_spmd(nc, in_maps, core_ids=list(range(8)))
    if KDBG:
        global _DBG_RES
        _DBG_RES = res

    full = np.empty((2, 8192, D), np.float32)
    for c in range(8):
        bb, si = divmod(c, 4)
        s0 = si * T
        full[bb, s0:s0 + T, :] = res.results[c]["out"].T
    return full


# version 07
# speedup vs baseline: 1.1229x; 1.1229x over previous
import contextlib
import numpy as np
import concourse.bass as bass
import concourse.mybir as mybir
from concourse.bass_utils import run_bass_kernel_spmd

F32 = mybir.dt.float32
BF16 = mybir.dt.bfloat16
AF = mybir.ActivationFunctionType
ALU = mybir.AluOpType
AX = mybir.AxisListType

P = 128
FC = 16
D = 2048
T = 2048
PRE = 64
W = 2176
ICOL = 2112
QSCALE = float(1.0 / np.sqrt(128.0))
SHIFT = -20.0
INVD = float(1.0 / 2048.0)
EPS = 1e-5
NEXP = 8
MOE_I = 1408
SH_I = 5632
CC_ALL = [(0, 512), (512, 512), (1024, 512), (1536, 512), (2048, 128)]
CC_MAIN = [(0, 512), (512, 512), (1024, 512), (1536, 512)]

import os
PHASES = int(os.environ.get("KPH", "99"))
KDBG = os.environ.get("KDBG", "")
KBLK = int(os.environ.get("KBLK", "4"))
KSKIP = os.environ.get("KSKIP", "")
KSEC = os.environ.get("KSEC", "abc")
KP6 = os.environ.get("KP6", "sg,gu")
KNJ = int(os.environ.get("KNJ", "44"))


class SemT:
    def __init__(self, nc, st, name):
        self.h = st.enter_context(nc.semaphore(name))
        self.n = 0

    def inc(self, instr, k=1):
        instr.then_inc(self.h, k)
        self.n += k
        return self.n


def wait_all(eng, sems, snap):
    for s in sems:
        if snap[s] > 0:
            eng.wait_ge(s.h, snap[s])


def build_nc():
    nc = bass.Bass("TRN2", target_bir_lowering=False)

    x_slab = nc.dram_tensor("x_slab", [D, W], F32, kind="ExternalInput").ap()
    wq_t = nc.dram_tensor("wq_t", [D, D], F32, kind="ExternalInput").ap()
    wk_t = nc.dram_tensor("wk_t", [D, 1024], F32, kind="ExternalInput").ap()
    wv_t = nc.dram_tensor("wv_t", [D, 1024], F32, kind="ExternalInput").ap()
    wo_t = nc.dram_tensor("wo_t", [D, D], F32, kind="ExternalInput").ap()
    gate_pt = nc.dram_tensor("gate_pt", [P, FC, 8], F32, kind="ExternalInput").ap()
    sg_pt = nc.dram_tensor("sg_pt", [P, FC, 1], F32, kind="ExternalInput").ap()
    gu_t = nc.dram_tensor("gu_t", [D, 2 * SH_I], F32, kind="ExternalInput").ap()
    dn_t = nc.dram_tensor("dn_t", [SH_I, D], F32, kind="ExternalInput").ap()
    w13_t = nc.dram_tensor("w13_t", [NEXP, D, 2 * MOE_I], F32, kind="ExternalInput").ap()
    w2_t = nc.dram_tensor("w2_t", [NEXP, MOE_I, D], F32, kind="ExternalInput").ap()
    maskF = nc.dram_tensor("maskF", [P, 256], F32, kind="ExternalInput").ap()
    maskM = nc.dram_tensor("maskM", [P, 256], F32, kind="ExternalInput").ap()
    maskFI = nc.dram_tensor("maskFI", [4, P], F32, kind="ExternalInput").ap()
    maskMI = nc.dram_tensor("maskMI", [4, P], F32, kind="ExternalInput").ap()
    eye_in = nc.dram_tensor("eye_in", [P, P], F32, kind="ExternalInput").ap()

    out = nc.dram_tensor("out", [D, T], F32, kind="ExternalOutput").ap()
    dbg = nc.dram_tensor("dbg", [17 * P, 1024], F32, kind="ExternalOutput").ap()
    ao_scr = nc.dram_tensor("ao_scr", [D, T], BF16, kind="Internal").ap()
    hs_scr = nc.dram_tensor("hs_scr", [SH_I, T], BF16, kind="Internal").ap()

    st = contextlib.ExitStack()
    with st:
        sQD = SemT(nc, st, "sQD")
        sHD = SemT(nc, st, "sHD")
        sSD = SemT(nc, st, "sSD")
        sPE = SemT(nc, st, "sPE")
        sDV = SemT(nc, st, "sDV")
        sAC = SemT(nc, st, "sAC")
        ALL = [sQD, sHD, sSD, sPE, sDV, sAC]

        def snap():
            return {s: s.n for s in ALL}

        def qdma(eng, sem, dst, srcap):
            if sem.n > 0:
                eng.wait_ge(sem.h, sem.n)
            eng.dma_start(dst, srcap).then_inc(sem.h, 16)
            sem.n += 16

        PB = [st.enter_context(nc.psum_tensor(f"PB{i}", [P, 512], F32)) for i in range(8)]

        ones_bf = st.enter_context(nc.sbuf_tensor("ones_bf", [P, 1], BF16))
        ones1_f = st.enter_context(nc.sbuf_tensor("ones1_f", [1, P], F32))
        maskF_sb = st.enter_context(nc.sbuf_tensor("maskF_sb", [P, 2, P], BF16))
        maskM_sb = st.enter_context(nc.sbuf_tensor("maskM_sb", [P, 2, P], BF16))
        maskFI_sb = st.enter_context(nc.sbuf_tensor("maskFI_sb", [4, P], BF16))
        maskMI_sb = st.enter_context(nc.sbuf_tensor("maskMI_sb", [4, P], BF16))
        eye_sb = st.enter_context(nc.sbuf_tensor("eye_sb", [P, P], F32))
        gate_sb = st.enter_context(nc.sbuf_tensor("gate_sb", [P, FC, 8], F32))
        sgw_sb = st.enter_context(nc.sbuf_tensor("sgw_sb", [P, FC, 1], BF16))
        shift_sb = st.enter_context(nc.sbuf_tensor("shift_sb", [P, 1], F32))

        block = st.enter_context(nc.Block())

        @block.gpsimd
        def _(g):
            qdma(g, sQD, maskF_sb[:, :, :], maskF)
            qdma(g, sQD, maskM_sb[:, :, :], maskM)
            qdma(g, sQD, maskFI_sb[:, :], maskFI)
            qdma(g, sQD, maskMI_sb[:, :], maskMI)
            qdma(g, sQD, eye_sb[:, :], eye_in)
            qdma(g, sQD, gate_sb[:, :, :], gate_pt)
            qdma(g, sQD, sgw_sb[:, :, :], sg_pt)

        @block.vector
        def _(v):
            v.memset(ones_bf[:, :], 1.0)
            v.memset(shift_sb[:, :], SHIFT)
            v.memset(ones1_f[:, :], 1.0).then_inc(sDV.h, 1)
        sDV.n += 1

        s13 = st.enter_context(contextlib.ExitStack())
        xn1 = s13.enter_context(nc.sbuf_tensor("xn1", [P, FC, W], BF16))

        if PHASES >= 1:
            s1 = contextlib.ExitStack()
            with s1:
                xin = [s1.enter_context(nc.sbuf_tensor(f"xin{i}", [P, FC, 512], F32)) for i in range(2)]
                xbf = s1.enter_context(nc.sbuf_tensor("xbf", [P, FC, 512], BF16))
                xsq = s1.enter_context(nc.sbuf_tensor("xsq", [P, FC, 512], BF16))
                t1 = s1.enter_context(nc.sbuf_tensor("t1", [P, 512], F32))
                rows = [s1.enter_context(nc.sbuf_tensor(f"r1_{k}", [1, 2, 512], F32))
                        for k in range(6)]
                Rm, Rq, Rmsq, Rsd, Rr, Rmr = rows
                b = snap()
                NC = len(CC_ALL)

                @block.sync
                def _(s):
                    for i, (c0, wn) in enumerate(CC_ALL):
                        if i >= 2:
                            s.wait_ge(sDV.h, b[sDV] + 3 * (i - 2) + 3)
                        src = x_slab[:, c0:c0 + wn].rearrange("(c p) t -> p c t", p=P)
                        qdma(s, sHD, xin[i % 2][:, :, 0:wn], src)

                @block.gpsimd
                def _(g):
                    for i, (c0, wn) in enumerate(CC_ALL):
                        if i >= 1:
                            g.wait_ge(sPE.h, b[sPE] + 2 * (i - 1) + 1)
                        src = x_slab[:, c0:c0 + wn].rearrange("(c p) t -> p c t", p=P)
                        qdma(g, sQD, xbf[:, :, 0:wn], src)

                @block.scalar
                def _(sc):
                    for i, (c0, wn) in enumerate(CC_ALL):
                        sc.wait_ge(sHD.h, b[sHD] + 16 * (i + 1))
                        if i >= 1:
                            sc.wait_ge(sPE.h, b[sPE] + 2 * (i - 1) + 1)
                        sc.activation(xsq[:, :, 0:wn], xin[i % 2][:, :, 0:wn], AF.Square).then_inc(sAC.h, 1)
                        sc.wait_ge(sDV.h, b[sDV] + 3 * i + 1)
                        sc.activation(Rsd[:, i % 2, 0:wn], Rq[:, i % 2, 0:wn], AF.Sqrt).then_inc(sAC.h, 1)
                sAC.n += 2 * NC

                @block.tensor
                def _(te):
                    te.wait_ge(sDV.h, b[sDV])
                    for i, (c0, wn) in enumerate(CC_ALL):
                        te.wait_ge(sQD.h, b[sQD] + 16 * (i + 1))
                        te.wait_ge(sAC.h, b[sAC] + 2 * i + 1)
                        if i >= 1:
                            te.wait_ge(sDV.h, b[sDV] + 3 * (i - 1) + 1)
                        for fc in range(FC):
                            te.matmul(PB[0][0:1, 0:wn], ones_bf[:, :], xbf[:, fc, 0:wn],
                                      start=(fc == 0), stop=(fc == FC - 1))
                        for fc in range(FC):
                            mm = te.matmul(PB[1][0:1, 0:wn], ones_bf[:, :], xsq[:, fc, 0:wn],
                                           start=(fc == 0), stop=(fc == FC - 1))
                        mm.then_inc(sPE.h, 1)
                        te.wait_ge(sDV.h, b[sDV] + 3 * i + 2)
                        if i >= 2:
                            te.wait_ge(sDV.h, b[sDV] + 3 * (i - 2) + 3)
                        te.matmul(PB[2 + i % 2][:, 0:wn], ones1_f[:, :], Rr[:, i % 2, 0:wn],
                                  start=True, stop=True)
                        mm = te.matmul(PB[4 + i % 2][:, 0:wn], ones1_f[:, :], Rmr[:, i % 2, 0:wn],
                                       start=True, stop=True)
                        mm.then_inc(sPE.h, 1)
                sPE.n += 2 * NC

                @block.vector
                def _(v):
                    for i, (c0, wn) in enumerate(CC_ALL):
                        ip = i % 2
                        v.wait_ge(sPE.h, b[sPE] + 2 * i + 1)
                        if i >= 2:
                            v.wait_ge(sAC.h, b[sAC] + 2 * (i - 2) + 2)
                            v.wait_ge(sPE.h, b[sPE] + 2 * (i - 2) + 2)
                        v.tensor_scalar(Rm[:, ip, 0:wn], PB[0][0:1, 0:wn], INVD, None, op0=ALU.mult)
                        v.tensor_scalar(Rq[:, ip, 0:wn], PB[1][0:1, 0:wn], INVD, None, op0=ALU.mult)
                        v.tensor_tensor(Rmsq[:, ip, 0:wn], Rm[:, ip, 0:wn], Rm[:, ip, 0:wn], op=ALU.mult)
                        v.tensor_tensor(Rq[:, ip, 0:wn], Rq[:, ip, 0:wn], Rmsq[:, ip, 0:wn],
                                        op=ALU.subtract)
                        v.tensor_scalar(Rq[:, ip, 0:wn], Rq[:, ip, 0:wn], EPS, None,
                                        op0=ALU.add).then_inc(sDV.h, 1)
                        v.wait_ge(sAC.h, b[sAC] + 2 * i + 2)
                        v.reciprocal(Rr[:, ip, 0:wn], Rsd[:, ip, 0:wn])
                        v.tensor_tensor(Rmr[:, ip, 0:wn], Rm[:, ip, 0:wn], Rr[:, ip, 0:wn],
                                        op=ALU.mult).then_inc(sDV.h, 1)
                        v.wait_ge(sPE.h, b[sPE] + 2 * i + 2)
                        for fc in range(FC):
                            v.tensor_tensor(t1[:, 0:wn], xin[ip][:, fc, 0:wn], PB[2 + ip][:, 0:wn], op=ALU.mult)
                            tt = v.tensor_tensor(xn1[:, fc, c0:c0 + wn], t1[:, 0:wn], PB[4 + ip][:, 0:wn],
                                                 op=ALU.subtract)
                        tt.then_inc(sDV.h, 1)
                sDV.n += 3 * NC

        if PHASES >= 2:
            k_sb = s13.enter_context(nc.sbuf_tensor("k_sb", [P, 8, W], BF16))
            vT_sb = s13.enter_context(nc.sbuf_tensor("vT_sb", [P, 17, 1024], BF16))
            vI_sb = s13.enter_context(nc.sbuf_tensor("vI_sb", [4, 1024], BF16))
            s2 = contextlib.ExitStack()
            with s2:
                wkbuf = [s2.enter_context(nc.sbuf_tensor(f"wkbuf{i}", [P, FC, P], BF16)) for i in range(2)]
                wvbuf = [s2.enter_context(nc.sbuf_tensor(f"wvbuf{i}", [P, FC, 512], BF16)) for i in range(2)]
                b = snap()
                NCC = len(CC_ALL)

                @block.gpsimd
                def _(g):
                    for ock in range(8):
                        if ock >= 2:
                            g.wait_ge(sPE.h, b[sPE] + NCC * (ock - 1))
                        else:
                            g.wait_ge(sPE.h, b[sPE])
                        src = wk_t[:, ock * P:(ock + 1) * P].rearrange("(c p) n -> p c n", p=P)
                        qdma(g, sQD, wkbuf[ock % 2][:, :, :], src)
                    for on in range(2):
                        if on >= 2:
                            g.wait_ge(sPE.h, b[sPE] + NCC * 8 + 18 * (on - 1))
                        src = wv_t[:, on * 512:(on + 1) * 512].rearrange("(c p) n -> p c n", p=P)
                        qdma(g, sQD, wvbuf[on % 2][:, :, :], src)

                @block.tensor
                def _(te):
                    wait_all(te, [sDV, sQD], b)
                    gi = 0
                    for ock in range(8):
                        te.wait_ge(sQD.h, b[sQD] + 16 * (ock + 1))
                        for cc, (c0, wn) in enumerate(CC_ALL):
                            par = gi % 2
                            if gi >= 2:
                                te.wait_ge(sDV.h, b[sDV] + gi - 1)
                            for fc in range(FC):
                                mm = te.matmul(PB[par][:, 0:wn], wkbuf[ock % 2][:, fc, :],
                                               xn1[:, fc, c0:c0 + wn],
                                               start=(fc == 0), stop=(fc == FC - 1))
                            mm.then_inc(sPE.h, 1)
                            gi += 1
                    for on in range(2):
                        te.wait_ge(sQD.h, b[sQD] + 16 * (9 + on))
                        for tt in range(17):
                            par = gi % 2
                            if gi >= 2:
                                te.wait_ge(sDV.h, b[sDV] + gi - 1)
                            for fc in range(FC):
                                mm = te.matmul(PB[par][:, :], xn1[:, fc, tt * P:(tt + 1) * P],
                                               wvbuf[on % 2][:, fc, :],
                                               start=(fc == 0), stop=(fc == FC - 1))
                            mm.then_inc(sPE.h, 1)
                            gi += 1
                        par = gi % 2
                        if gi >= 2:
                            te.wait_ge(sDV.h, b[sDV] + gi - 1)
                        for fc in range(FC):
                            mm = te.matmul(PB[par][0:4, :], xn1[:, fc, ICOL:ICOL + 4],
                                           wvbuf[on % 2][:, fc, :],
                                           start=(fc == 0), stop=(fc == FC - 1))
                        mm.then_inc(sPE.h, 1)
                        gi += 1
                sPE.n += 8 * NCC + 2 * 18

                @block.vector
                def _(v):
                    gi = 0
                    for ock in range(8):
                        for cc, (c0, wn) in enumerate(CC_ALL):
                            v.wait_ge(sPE.h, b[sPE] + gi + 1)
                            v.tensor_copy(k_sb[:, ock, c0:c0 + wn], PB[gi % 2][:, 0:wn]).then_inc(sDV.h, 1)
                            gi += 1
                    for on in range(2):
                        for tt in range(17):
                            v.wait_ge(sPE.h, b[sPE] + gi + 1)
                            v.tensor_copy(vT_sb[:, tt, on * 512:(on + 1) * 512],
                                          PB[gi % 2][:, :]).then_inc(sDV.h, 1)
                            gi += 1
                        v.wait_ge(sPE.h, b[sPE] + gi + 1)
                        v.tensor_copy(vI_sb[0:4, on * 512:(on + 1) * 512],
                                      PB[gi % 2][0:4, :]).then_inc(sDV.h, 1)
                        gi += 1
                sDV.n += 8 * NCC + 2 * 18

        if PHASES >= 3:
            s3 = contextlib.ExitStack()
            with s3:
                wqg = [s3.enter_context(nc.sbuf_tensor(f"wqg{i}", [P, FC, 256], BF16)) for i in range(2)]
                q_blk = s3.enter_context(nc.sbuf_tensor("q_blk", [P, FC, 512], BF16))
                av_blk = s3.enter_context(nc.sbuf_tensor("av_blk", [P, FC, 512], BF16))
                pt_sb = [s3.enter_context(nc.sbuf_tensor(f"pt{i}", [P, 2, P], BF16)) for i in range(3)]
                ptI_sb = [s3.enter_context(nc.sbuf_tensor(f"ptI{i}", [4, P], BF16)) for i in range(3)]
                rden = [s3.enter_context(nc.sbuf_tensor(f"rden{i}", [1, P], F32)) for i in range(3)]
                denB = [s3.enter_context(nc.sbuf_tensor(f"denB{i}", [P, P], F32)) for i in range(3)]
                ao_st = [s3.enter_context(nc.sbuf_tensor(f"ao_st{i}", [P, 512], BF16)) for i in range(2)]
                dummy = s3.enter_context(nc.sbuf_tensor("dmy_sb", [P, 512], F32))
                dmy = s3.enter_context(nc.sbuf_tensor("dmy_pt", [P, 2, P], BF16))
                dmy_wq = s3.enter_context(nc.sbuf_tensor("dmy_wq", [P, FC, 256], BF16))
                dmyI = s3.enter_context(nc.sbuf_tensor("dmy_ptI", [4, P], BF16))

                SCB = [2, 3, 6]
                AVB = [4, 5, 7]

                @block.vector
                def _(v):
                    v.memset(pt_sb[0][:, :, :], 0.0)
                    v.memset(pt_sb[1][:, :, :], 0.0)
                    v.memset(pt_sb[2][:, :, :], 0.0).then_inc(sDV.h, 1)
                sDV.n += 1

                for bi in range(KBLK):
                    b = snap()

                    @block.gpsimd
                    def _(g, bi=bi, b=b):
                        for ocg in range(8):
                            if ocg >= 2:
                                g.wait_ge(sPE.h, b[sPE] + 2 * (ocg - 1))
                            else:
                                g.wait_ge(sPE.h, b[sPE])
                            src = wq_t[:, ocg * 256:(ocg + 1) * 256].rearrange("(c p) n -> p c n", p=P)
                            if KSKIP == "qdma":
                                qdma(g, sQD, dmy_wq[:, :, :], src)
                            else:
                                qdma(g, sQD, wqg[ocg % 2][:, :, :], src)

                    @block.tensor
                    def _(te, bi=bi, b=b):
                        te.wait_ge(sDV.h, b[sDV])
                        bc0 = 64 + bi * 512
                        for ocg in range(8):
                            te.wait_ge(sQD.h, b[sQD] + 16 * (ocg + 1))
                            for oi in range(2):
                                oc = ocg * 2 + oi
                                par = oc % 2
                                if oc >= 2:
                                    te.wait_ge(sDV.h, b[sDV] + oc - 1)
                                if KSKIP == "qmm":
                                    mm = te.matmul(PB[par][:, :], dmy_wq[:, 0, 0:P],
                                                   xn1[:, 0, bc0:bc0 + 512], start=True, stop=True)
                                else:
                                    for fc in range(FC):
                                        mm = te.matmul(PB[par][:, :], wqg[ocg % 2][:, fc, oi * P:(oi + 1) * P],
                                                       xn1[:, fc, bc0:bc0 + 512],
                                                       start=(fc == 0), stop=(fc == FC - 1))
                                mm.then_inc(sPE.h, 1)
                    sPE.n += 16

                    @block.vector
                    def _(v, bi=bi, b=b):
                        for oc in range(16):
                            v.wait_ge(sPE.h, b[sPE] + oc + 1)
                            _qdst = dummy[:, :].bitcast(BF16)[:, 0:512] if KSKIP == "qevac" else q_blk[:, oc, :]
                            v.tensor_scalar(_qdst, PB[oc % 2][:, :], QSCALE, None,
                                            op0=ALU.mult).then_inc(sDV.h, 1)
                    sDV.n += 16

                    if "b" not in KSEC:
                        continue
                    b = snap()
                    NU = 64
                    pe_scores = [0] * NU
                    pe_den = [0] * NU
                    pe_av = [0] * NU
                    pe_denB = [0] * NU
                    c = b[sPE]
                    for u in range(NU + 1):
                        if u < NU:
                            c += 1
                            pe_scores[u] = c
                        if u >= 1:
                            c += 1
                            pe_den[u - 1] = c
                            c += 1
                            pe_av[u - 1] = c
                            c += 1
                            pe_denB[u - 1] = c
                    pe_total = c
                    dv_mask = [0] * NU
                    dv_rden = [0] * NU
                    dv_denBe = [0] * NU
                    dv_stt = [0] * NU
                    c = b[sDV]
                    for u in range(NU):
                        c += 1
                        dv_mask[u] = c
                        c += 1
                        dv_rden[u] = c
                        c += 1
                        dv_denBe[u] = c
                        c += 1
                        dv_stt[u] = c
                    dv_total = c
                    ac_exp = [b[sAC] + u + 1 for u in range(NU)]

                    first_mask = (bi == 0)

                    @block.tensor
                    def _(te, bi=bi, b=b, pe_scores=pe_scores, dv_mask=dv_mask,
                          dv_rden=dv_rden, dv_denBe=dv_denBe, dv_stt=dv_stt, ac_exp=ac_exp):
                        te.wait_ge(sDV.h, b[sDV])
                        for u in range(NU + 1):
                            if u < NU:
                                qtl, h = divmod(u, 16)
                                qt = bi * 4 + qtl
                                par = u % 3
                                kvc = h // 2
                                kc0 = qt * P
                                if u >= 3:
                                    te.wait_ge(sAC.h, ac_exp[u - 3])
                                    te.wait_ge(sDV.h, dv_denBe[u - 3])
                                te.matmul(PB[SCB[par]][:, 0:128], k_sb[:, kvc, kc0:kc0 + P],
                                          q_blk[:, h, qtl * P:(qtl + 1) * P], start=True, stop=True)
                                te.matmul(PB[SCB[par]][0:64, 128:256], k_sb[:, kvc, kc0 + P:kc0 + P + 64],
                                          q_blk[:, h, qtl * P:(qtl + 1) * P], start=True, stop=True)
                                mm = te.matmul(PB[SCB[par]][0:4, 256:384], k_sb[:, kvc, ICOL:ICOL + 4],
                                               q_blk[:, h, qtl * P:(qtl + 1) * P], start=True, stop=True)
                                mm.then_inc(sPE.h, 1)
                            if u >= 1:
                                v = u - 1
                                qtl, h = divmod(v, 16)
                                qt = bi * 4 + qtl
                                par = v % 3
                                kvc = h // 2
                                te.wait_ge(sDV.h, dv_mask[v])
                                te.matmul(PB[SCB[par]][0:1, 384:512], ones_bf[:, :], pt_sb[par][:, 0, :],
                                          start=True, stop=False)
                                te.matmul(PB[SCB[par]][0:1, 384:512], ones_bf[:, :], pt_sb[par][:, 1, :],
                                          start=False, stop=False)
                                mm = te.matmul(PB[SCB[par]][0:1, 384:512], ones_bf[0:4, :], ptI_sb[par][0:4, :],
                                               start=False, stop=True)
                                mm.then_inc(sPE.h, 1)
                                if v >= 3:
                                    te.wait_ge(sDV.h, dv_stt[v - 3])
                                te.matmul(PB[AVB[par]][:, 0:128], vT_sb[:, qt, kvc * P:(kvc + 1) * P],
                                          pt_sb[par][:, 0, :], start=True, stop=False)
                                te.matmul(PB[AVB[par]][:, 0:128], vT_sb[0:64, qt + 1, kvc * P:(kvc + 1) * P],
                                          pt_sb[par][0:64, 1, :], start=False, stop=False)
                                mm = te.matmul(PB[AVB[par]][:, 0:128], vI_sb[0:4, kvc * P:(kvc + 1) * P],
                                               ptI_sb[par][0:4, :], start=False, stop=True)
                                mm.then_inc(sPE.h, 1)
                                te.wait_ge(sDV.h, dv_rden[v])
                                mm = te.matmul(PB[AVB[par]][:, 128:256], ones1_f[:, :], rden[par][:, :],
                                               start=True, stop=True)
                                mm.then_inc(sPE.h, 1)

                    @block.scalar
                    def _(sc, b=b, pe_scores=pe_scores, pe_av=pe_av):
                        for u in range(NU):
                            par = u % 3
                            sc.wait_ge(sPE.h, pe_scores[u])
                            if u >= 3:
                                sc.wait_ge(sPE.h, pe_av[u - 3])
                            if KSKIP == "exp":
                                _d1, _d2, _d3 = (dmy[:, 0, :], dmy[0:64, 1, :], dmyI[0:4, :])
                            else:
                                _d1, _d2, _d3 = (pt_sb[par][:, 0, :], pt_sb[par][0:64, 1, :], ptI_sb[par][0:4, :])
                            sc.activation(_d1, PB[SCB[par]][:, 0:128], AF.Exp, bias=shift_sb[:, :])
                            sc.activation(_d2, PB[SCB[par]][0:64, 128:256], AF.Exp, bias=shift_sb[0:64, :])
                            sc.activation(_d3, PB[SCB[par]][0:4, 256:384], AF.Exp,
                                          bias=shift_sb[0:4, :]).then_inc(sAC.h, 1)
                    sAC.n += NU

                    @block.vector
                    def _(v, bi=bi, b=b, ac_exp=ac_exp, pe_den=pe_den, pe_av=pe_av,
                          pe_denB=pe_denB, first_mask=first_mask):
                        for u in range(NU):
                            qtl, h = divmod(u, 16)
                            qt = bi * 4 + qtl
                            par = u % 3
                            msk = maskF_sb if (first_mask and qtl == 0) else maskM_sb
                            mski = maskFI_sb if (first_mask and qtl == 0) else maskMI_sb
                            v.wait_ge(sAC.h, ac_exp[u])
                            v.tensor_tensor(pt_sb[par][:, :, :], pt_sb[par][:, :, :], msk[:, :, :],
                                            op=ALU.mult)
                            v.tensor_tensor(ptI_sb[par][0:4, :], ptI_sb[par][0:4, :], mski[0:4, :],
                                            op=ALU.mult).then_inc(sDV.h, 1)
                            v.wait_ge(sPE.h, pe_den[u])
                            v.reciprocal(rden[par][:, :], PB[SCB[par]][0:1, 384:512]).then_inc(sDV.h, 1)
                            v.wait_ge(sPE.h, pe_denB[u])
                            _dbedst = dummy[:, 0:128] if KSKIP == "denbe" else denB[par][:, :]
                            v.tensor_copy(_dbedst, PB[AVB[par]][:, 128:256]).then_inc(sDV.h, 1)
                            v.wait_ge(sPE.h, pe_av[u])
                            _sttdst = dummy[:, 0:128].bitcast(BF16)[:, 0:128] if KSKIP == "stt" else av_blk[:, h, qtl * P:(qtl + 1) * P]
                            v.scalar_tensor_tensor(_sttdst, PB[AVB[par]][:, 0:128],
                                                   1.0, denB[par][:, :], op0=ALU.mult,
                                                   op1=ALU.mult).then_inc(sDV.h, 1)
                    sDV.n += 4 * NU
                    assert sPE.n + (4 * NU) == pe_total
                    sPE.n = pe_total
                    assert sDV.n == dv_total

                    if "c" not in KSEC:
                        continue
                    b = snap()

                    @block.gpsimd
                    def _(g, bi=bi, b=b):
                        for ocg in range(8):
                            if ocg >= 2:
                                g.wait_ge(sPE.h, b[sPE] + 2 * (ocg - 1))
                            else:
                                g.wait_ge(sPE.h, b[sPE] - 4 * 64)
                            src = wo_t[:, ocg * 256:(ocg + 1) * 256].rearrange("(c p) n -> p c n", p=P)
                            qdma(g, sQD, wqg[ocg % 2][:, :, :], src)

                    @block.tensor
                    def _(te, bi=bi, b=b):
                        te.wait_ge(sDV.h, b[sDV])
                        for ocg in range(8):
                            te.wait_ge(sQD.h, b[sQD] + 16 * (ocg + 1))
                            for oi in range(2):
                                oc = ocg * 2 + oi
                                par = oc % 2
                                if oc >= 2:
                                    te.wait_ge(sDV.h, b[sDV] + oc - 1)
                                for fc in range(FC):
                                    mm = te.matmul(PB[par][:, :], wqg[ocg % 2][:, fc, oi * P:(oi + 1) * P],
                                                   av_blk[:, fc, :],
                                                   start=(fc == 0), stop=(fc == FC - 1))
                                mm.then_inc(sPE.h, 1)
                    sPE.n += 16

                    @block.vector
                    def _(v, bi=bi, b=b):
                        for oc in range(16):
                            v.wait_ge(sPE.h, b[sPE] + oc + 1)
                            if oc >= 2:
                                v.wait_ge(sSD.h, b[sSD] + 16 * (oc - 1))
                            v.tensor_copy(ao_st[oc % 2][:, :], PB[oc % 2][:, :]).then_inc(sDV.h, 1)
                    sDV.n += 16

                    @block.scalar
                    def _(sc, bi=bi, b=b):
                        for oc in range(16):
                            sc.wait_ge(sDV.h, b[sDV] + oc + 1)
                            dst = ao_scr[oc * P:(oc + 1) * P, bi * 512:(bi + 1) * 512]
                            qdma(sc, sSD, dst, ao_st[oc % 2][:, :])

        s13.close()

        s58 = st.enter_context(contextlib.ExitStack())
        xn2 = s58.enter_context(nc.sbuf_tensor("xn2", [P, FC, T], BF16))
        w8t_sb = s58.enter_context(nc.sbuf_tensor("w8t_sb", [8, T], F32))
        s45 = st.enter_context(contextlib.ExitStack())
        R2m = s45.enter_context(nc.sbuf_tensor("R2m", [1, T], F32, side="right"))
        R2r = s45.enter_context(nc.sbuf_tensor("R2r", [1, T], F32, side="right"))
        R2mr = s45.enter_context(nc.sbuf_tensor("R2mr", [1, T], F32, side="right"))
        if PHASES >= 4:
            s4 = contextlib.ExitStack()
            with s4:
                aob = s4.enter_context(nc.sbuf_tensor("aob", [P, FC, 512], BF16))
                x1b = s4.enter_context(nc.sbuf_tensor("x1b", [P, FC, 512], F32))
                x1bf = s4.enter_context(nc.sbuf_tensor("x1bf", [P, FC, 512], BF16))
                Rq4 = s4.enter_context(nc.sbuf_tensor("Rq4", [1, 2, 512], F32, side="right"))
                Rmsq4 = s4.enter_context(nc.sbuf_tensor("Rmsq4", [1, 2, 512], F32, side="right"))
                Rsd4 = s4.enter_context(nc.sbuf_tensor("Rsd4", [1, 2, 512], F32, side="right"))
                b = snap()

                @block.sync
                def _(s):
                    wait_all(s, [sSD], b)
                    for i in range(4):
                        if i >= 1:
                            s.wait_ge(sSD.h, b[sSD] + 16 * i)
                            s.wait_ge(sAC.h, b[sAC] + 2 * (i - 1) + 1)
                        c0 = 64 + i * 512
                        src = x_slab[:, c0:c0 + 512].rearrange("(c p) t -> p c t", p=P)
                        qdma(s, sHD, x1b[:, :, :], src)

                @block.gpsimd
                def _(g):
                    wait_all(g, [sSD], b)
                    for i in range(4):
                        if i >= 1:
                            g.wait_ge(sDV.h, b[sDV] + 4 * (i - 1) + 1)
                        src = ao_scr[:, i * 512:(i + 1) * 512].rearrange("(c p) t -> p c t", p=P)
                        qdma(g, sQD, aob[:, :, :], src)

                @block.vector
                def _(v):
                    for i in range(4):
                        ip = i % 2
                        v.wait_ge(sHD.h, b[sHD] + 16 * (i + 1))
                        v.wait_ge(sQD.h, b[sQD] + 16 * (i + 1))
                        for fc in range(FC):
                            tt = v.tensor_tensor(x1b[:, fc, :], x1b[:, fc, :], aob[:, fc, :], op=ALU.add)
                        tt.then_inc(sDV.h, 1)
                        if i >= 1:
                            v.wait_ge(sPE.h, b[sPE] + 2 * i)
                        v.tensor_copy(x1bf[:, :, :], x1b[:, :, :]).then_inc(sDV.h, 1)
                        v.wait_ge(sPE.h, b[sPE] + 2 * i + 2)
                        if i >= 2:
                            v.wait_ge(sAC.h, b[sAC] + 2 * (i - 2) + 2)
                        v.tensor_scalar(R2m[:, i * 512:(i + 1) * 512], PB[0][0:1, :], INVD, None, op0=ALU.mult)
                        v.tensor_scalar(Rq4[:, ip, :], PB[1][0:1, :], INVD, None, op0=ALU.mult)
                        v.tensor_tensor(Rmsq4[:, ip, :], R2m[:, i * 512:(i + 1) * 512],
                                        R2m[:, i * 512:(i + 1) * 512], op=ALU.mult)
                        v.tensor_tensor(Rq4[:, ip, :], Rq4[:, ip, :], Rmsq4[:, ip, :],
                                        op=ALU.subtract)
                        v.tensor_scalar(Rq4[:, ip, :], Rq4[:, ip, :], EPS, None,
                                        op0=ALU.add).then_inc(sDV.h, 1)
                        v.wait_ge(sAC.h, b[sAC] + 2 * i + 2)
                        v.reciprocal(R2r[:, i * 512:(i + 1) * 512], Rsd4[:, ip, :])
                        v.tensor_tensor(R2mr[:, i * 512:(i + 1) * 512], R2m[:, i * 512:(i + 1) * 512],
                                        R2r[:, i * 512:(i + 1) * 512], op=ALU.mult).then_inc(sDV.h, 1)
                sDV.n += 4 * 4

                @block.scalar
                def _(sc):
                    for i in range(4):
                        ip = i % 2
                        sc.wait_ge(sPE.h, b[sPE] + 2 * i + 1)
                        sc.activation(x1bf[:, :, :], x1b[:, :, :], AF.Square).then_inc(sAC.h, 1)
                        sc.wait_ge(sDV.h, b[sDV] + 4 * i + 3)
                        sc.activation(Rsd4[:, ip, :], Rq4[:, ip, :], AF.Sqrt).then_inc(sAC.h, 1)
                        qdma(sc, sSD, out[:, i * 512:(i + 1) * 512].rearrange("(c p) t -> p c t", p=P),
                                     x1b[:, :, :])
                sAC.n += 2 * 4

                @block.tensor
                def _(te):
                    for i in range(4):
                        te.wait_ge(sDV.h, b[sDV] + 4 * i + 2)
                        if i >= 1:
                            te.wait_ge(sDV.h, b[sDV] + 4 * (i - 1) + 3)
                        for fc in range(FC):
                            mm = te.matmul(PB[0][0:1, :], ones_bf[:, :], x1bf[:, fc, :],
                                           start=(fc == 0), stop=(fc == FC - 1))
                        mm.then_inc(sPE.h, 1)
                        te.wait_ge(sAC.h, b[sAC] + 2 * i + 1)
                        for fc in range(FC):
                            mm = te.matmul(PB[1][0:1, :], ones_bf[:, :], x1bf[:, fc, :],
                                           start=(fc == 0), stop=(fc == FC - 1))
                        mm.then_inc(sPE.h, 1)
                sPE.n += 2 * 4

        if PHASES >= 5:
            s5 = contextlib.ExitStack()
            with s5:
                x1in = s5.enter_context(nc.sbuf_tensor("x1in", [P, FC, 512], F32))
                xnf = [s5.enter_context(nc.sbuf_tensor(f"xnf{i}", [P, FC, 512], F32)) for i in range(2)]
                t15 = s5.enter_context(nc.sbuf_tensor("t15", [P, 512], F32))
                l_sb = s5.enter_context(nc.sbuf_tensor("l_sb", [P, 128], F32))
                w8_sb = s5.enter_context(nc.sbuf_tensor("w8_sb", [P, 128], F32))
                e8 = s5.enter_context(nc.sbuf_tensor("e8", [P, 8], F32))
                p8 = s5.enter_context(nc.sbuf_tensor("p8", [P, 8], F32))
                sc8 = s5.enter_context(nc.sbuf_tensor("sc8", [P, 6], F32))
                rdbg = s5.enter_context(nc.sbuf_tensor("rdbg", [P, 32], F32))
                b = snap()

                @block.sync
                def _(s):
                    wait_all(s, [sSD], b)
                    for cc in range(4):
                        if cc >= 1:
                            s.wait_ge(sDV.h, b[sDV] + 42 * (cc - 1) + 1)
                        qdma(s, sHD, x1in[:, :, :],
                                    out[:, cc * 512:(cc + 1) * 512].rearrange("(c p) t -> p c t", p=P)
                                    )

                @block.tensor
                def _(te):
                    for cc in range(4):
                        par = cc % 2
                        te.wait_ge(sDV.h, b[sDV])
                        if cc >= 2:
                            te.wait_ge(sDV.h, b[sDV] + 42 * (cc - 2) + 1)
                        te.matmul(PB[0 + par][:, :], ones1_f[:, :], R2r[:, cc * 512:(cc + 1) * 512],
                                  start=True, stop=True)
                        mm = te.matmul(PB[2 + par][:, :], ones1_f[:, :], R2mr[:, cc * 512:(cc + 1) * 512],
                                       start=True, stop=True)
                        mm.then_inc(sPE.h, 1)
                        te.wait_ge(sDV.h, b[sDV] + 42 * cc + 1)
                        if cc >= 2:
                            te.wait_ge(sDV.h, b[sDV] + 42 * (cc - 2) + 2)
                        for j in range(4):
                            for fc in range(FC):
                                mm = te.matmul(PB[4 + par][:, j * 8:(j + 1) * 8],
                                               xnf[par][:, fc, j * P:(j + 1) * P],
                                               gate_sb[:, fc, :],
                                               start=(fc == 0), stop=(fc == FC - 1))
                        mm.then_inc(sPE.h, 1)
                sPE.n += 2 * 4

                @block.vector
                def _(v):
                    dvc = b[sDV]
                    for cc in range(4):
                        par = cc % 2
                        v.wait_ge(sHD.h, b[sHD] + 16 * (cc + 1))
                        v.wait_ge(sPE.h, b[sPE] + 2 * cc + 1)
                        for fc in range(FC):
                            v.tensor_tensor(t15[:, :], x1in[:, fc, :], PB[0 + par][:, :], op=ALU.mult)
                            tt = v.tensor_tensor(xnf[par][:, fc, :], t15[:, :], PB[2 + par][:, :],
                                                 op=ALU.subtract)
                        tt.then_inc(sDV.h, 1)
                        dvc += 1
                        v.tensor_copy(xn2[:, :, cc * 512:(cc + 1) * 512], xnf[par][:, :, :])
                        v.wait_ge(sPE.h, b[sPE] + 2 * cc + 2)
                        v.tensor_copy(l_sb[:, cc * 32:(cc + 1) * 32], PB[4 + par][:, 0:32]).then_inc(sDV.h, 1)
                        dvc += 1
                        for j in range(cc * 4, cc * 4 + 4):
                            lj = l_sb[:, j * 8:(j + 1) * 8]

                            def _st(instr):
                                nonlocal dvc
                                instr.then_inc(sDV.h, 1)
                                dvc += 1
                                v.wait_ge(sDV.h, dvc)
                            _st(v.reduce_max(sc8[:, 0:1], lj, axis=AX.X))
                            _st(v.reduce_max(sc8[:, 1:2], lj, axis=AX.X, negate=True))
                            _st(v.tensor_tensor(e8[:, :], lj, sc8[:, 0:1].broadcast_to([P, 8]),
                                                op=ALU.is_ge))
                            _st(v.scalar_tensor_tensor(p8[:, :], e8[:, :], -60.0, lj,
                                                       op0=ALU.mult, op1=ALU.add))
                            _st(v.reduce_max(sc8[:, 2:3], p8[:, :], axis=AX.X))
                            _st(v.tensor_tensor(e8[:, :], lj, sc8[:, 2:3].broadcast_to([P, 8]),
                                                op=ALU.is_ge))
                            v.wait_ge(sAC.h, b[sAC] + j + 1)
                            _st(v.reduce_sum(sc8[:, 3:4], p8[:, :], axis=AX.X))
                            _st(v.reciprocal(sc8[:, 4:5], sc8[:, 3:4]))
                            _st(v.tensor_tensor(p8[:, :], p8[:, :], sc8[:, 4:5].broadcast_to([P, 8]),
                                                op=ALU.mult))
                            _st(v.tensor_tensor(w8_sb[:, j * 8:(j + 1) * 8], p8[:, :], e8[:, :],
                                                op=ALU.mult))
                sDV.n += 4 * 42

                @block.scalar
                def _(sc):
                    for cc in range(4):
                        for j in range(cc * 4, cc * 4 + 4):
                            sc.wait_ge(sDV.h, b[sDV] + 42 * cc + 2 + (j - cc * 4) * 10 + 5)
                            sc.activation(p8[:, :], l_sb[:, j * 8:(j + 1) * 8], AF.Exp,
                                          bias=sc8[:, 1:2]).then_inc(sAC.h, 1)
                sAC.n += 16

                b2 = snap()

                @block.tensor
                def _(te):
                    wait_all(te, [sDV], b2)
                    for j in range(16):
                        if j >= 1:
                            te.wait_ge(sDV.h, b2[sDV] + j)
                        mm = te.transpose(PB[6][0:8, 0:128], w8_sb[:, j * 8:(j + 1) * 8], eye_sb[:, :])
                        mm.then_inc(sPE.h, 1)
                sPE.n += 16

                @block.vector
                def _(v):
                    for j in range(16):
                        v.wait_ge(sPE.h, b2[sPE] + j + 1)
                        v.tensor_copy(w8t_sb[0:8, j * P:(j + 1) * P], PB[6][0:8, 0:128]).then_inc(sDV.h, 1)
                sDV.n += 16

        s45.close()
        s67 = st.enter_context(contextlib.ExitStack())
        sigB_sb = s67.enter_context(nc.sbuf_tensor("sigB_sb", [P, T], F32, side="right"))

        if PHASES >= 6:
            s6 = contextlib.ExitStack()
            with s6:
                sig_row = s6.enter_context(nc.sbuf_tensor("sig_row", [1, T], F32))
                gub = [s6.enter_context(nc.sbuf_tensor(f"gub{i}", [P, FC, 256], BF16)) for i in range(2)]
                hs_st = [s6.enter_context(nc.sbuf_tensor(f"hs_st{i}", [P, T], BF16)) for i in range(2)]
                s_act = [s6.enter_context(nc.sbuf_tensor(f"s_act{i}", [P, 512], F32)) for i in range(2)]
                b = snap()

                if "sg" in KP6:
                    @block.tensor
                    def _(te):
                        wait_all(te, [sDV], b)
                        for cc in range(4):
                            par = cc % 2
                            if cc >= 2:
                                te.wait_ge(sAC.h, b[sAC] + cc - 1)
                            for fc in range(FC):
                                mm = te.matmul(PB[4 + par][0:1, :], sgw_sb[:, fc, :],
                                               xn2[:, fc, cc * 512:(cc + 1) * 512],
                                               start=(fc == 0), stop=(fc == FC - 1))
                            mm.then_inc(sPE.h, 1)
                        for cc in range(4):
                            te.wait_ge(sAC.h, b[sAC] + cc + 1)
                            if cc >= 1:
                                te.wait_ge(sDV.h, b[sDV] + cc)
                            mm = te.matmul(PB[6][:, :], ones1_f[:, :], sig_row[:, cc * 512:(cc + 1) * 512],
                                           start=True, stop=True)
                            mm.then_inc(sPE.h, 1)
                    sPE.n += 8

                    @block.scalar
                    def _(sc):
                        for cc in range(4):
                            par = cc % 2
                            sc.wait_ge(sPE.h, b[sPE] + cc + 1)
                            sc.activation(sig_row[:, cc * 512:(cc + 1) * 512], PB[4 + par][0:1, :],
                                          AF.Sigmoid).then_inc(sAC.h, 1)
                    sAC.n += 4

                    @block.vector
                    def _(v):
                        for cc in range(4):
                            v.wait_ge(sPE.h, b[sPE] + 4 + cc + 1)
                            v.tensor_copy(sigB_sb[:, cc * 512:(cc + 1) * 512], PB[6][:, :]).then_inc(sDV.h, 1)
                    sDV.n += 4

                GU = "gu" in KP6
                b = snap()
                NJ = KNJ

                if GU:
                  @block.gpsimd
                  def _(g):
                    for j in range(NJ):
                        if j >= 2:
                            g.wait_ge(sPE.h, b[sPE] + 4 * (j - 1))
                        else:
                            g.wait_ge(sPE.h, b[sPE])
                        src1 = gu_t[:, j * 256:(j + 1) * 256].rearrange("(c p) n -> p c n", p=P)
                        qdma(g, sQD, gub[j % 2][:, :, :], src1)

                if GU:
                  @block.tensor
                  def _(te):
                    for j in range(NJ):
                        te.wait_ge(sQD.h, b[sQD] + 16 * (j + 1))
                        for tc in range(4):
                            gi = j * 4 + tc
                            par = gi % 2
                            if gi >= 2:
                                te.wait_ge(sAC.h, b[sAC] + gi - 1)
                                te.wait_ge(sDV.h, b[sDV] + gi - 1)
                            for fc in range(FC):
                                te.matmul(PB[0 + par][:, :], gub[j % 2][:, fc, 0:P],
                                          xn2[:, fc, tc * 512:(tc + 1) * 512],
                                          start=(fc == 0), stop=(fc == FC - 1))
                            for fc in range(FC):
                                mm = te.matmul(PB[2 + par][:, :], gub[j % 2][:, fc, P:256],
                                               xn2[:, fc, tc * 512:(tc + 1) * 512],
                                               start=(fc == 0), stop=(fc == FC - 1))
                            mm.then_inc(sPE.h, 1)
                if GU:
                    sPE.n += 4 * NJ

                if GU:
                  @block.scalar
                  def _(sc):
                    for j in range(NJ):
                        for tc in range(4):
                            gi = j * 4 + tc
                            par = gi % 2
                            sc.wait_ge(sPE.h, b[sPE] + gi + 1)
                            if gi >= 2:
                                sc.wait_ge(sDV.h, b[sDV] + gi - 1)
                            sc.activation(s_act[par][:, :], PB[0 + par][:, :],
                                          AF.Silu).then_inc(sAC.h, 1)
                        sc.wait_ge(sDV.h, b[sDV] + 4 * (j + 1))
                        qdma(sc, sSD, hs_scr[j * P:(j + 1) * P, :], hs_st[j % 2][:, :])
                if GU:
                    sAC.n += 4 * NJ

                if GU:
                  @block.vector
                  def _(v):
                    for j in range(NJ):
                        for tc in range(4):
                            gi = j * 4 + tc
                            par = gi % 2
                            v.wait_ge(sAC.h, b[sAC] + gi + 1)
                            if j >= 2 and tc == 0:
                                v.wait_ge(sSD.h, b[sSD] + 16 * (j - 1))
                            v.tensor_tensor(hs_st[j % 2][:, tc * 512:(tc + 1) * 512], PB[2 + par][:, :],
                                            s_act[par][:, :], op=ALU.mult).then_inc(sDV.h, 1)
                if GU:
                    sDV.n += 4 * NJ



        if PHASES >= 7:
            s7 = contextlib.ExitStack()
            with s7:
                hs_sl = s7.enter_context(nc.sbuf_tensor("hs_sl", [P, 44, 1024], BF16))
                dnb = [s7.enter_context(nc.sbuf_tensor(f"dnb{i}", [P, 44, P], BF16)) for i in range(2)]
                o_t = [s7.enter_context(nc.sbuf_tensor(f"o_t{i}", [P, 512], F32)) for i in range(4)]
                tmp7 = [s7.enter_context(nc.sbuf_tensor(f"tmp7{i}", [P, 512], F32)) for i in range(2)]
                for hf in range(2):
                    b = snap()

                    @block.gpsimd
                    def _(g, hf=hf, b=b):
                        wait_all(g, [sSD, sPE], b)
                        for piece in range(4):
                            src = hs_scr[piece * 1408:(piece + 1) * 1408, hf * 1024:(hf + 1) * 1024]
                            src = src.rearrange("(c p) t -> p c t", p=P)
                            qdma(g, sQD, hs_sl[:, piece * 11:(piece + 1) * 11, :], src)
                        for oc in range(16):
                            if oc >= 2:
                                g.wait_ge(sPE.h, b[sPE] + 2 * (oc - 1))
                            src = dn_t[:, oc * P:(oc + 1) * P].rearrange("(c p) n -> p c n", p=P)
                            qdma(g, sQD, dnb[oc % 2][:, :, :], src)

                    @block.tensor
                    def _(te, hf=hf, b=b):
                        te.wait_ge(sQD.h, b[sQD] + 16 * 4)
                        for oc in range(16):
                            te.wait_ge(sQD.h, b[sQD] + 16 * (4 + oc + 1))
                            for tc2 in range(2):
                                gi = oc * 2 + tc2
                                par = gi % 4
                                if gi >= 4:
                                    te.wait_ge(sDV.h, b[sDV] + 2 * (gi - 3) - 1)
                                for mc in range(44):
                                    mm = te.matmul(PB[par][:, :], dnb[oc % 2][:, mc, :],
                                                   hs_sl[:, mc, tc2 * 512:(tc2 + 1) * 512],
                                                   start=(mc == 0), stop=(mc == 43))
                                mm.then_inc(sPE.h, 1)
                    sPE.n += 32

                    @block.sync
                    def _(s, hf=hf, b=b):
                        wait_all(s, [sSD], b)
                        for gi in range(32):
                            oc, tc2 = divmod(gi, 2)
                            if gi >= 4:
                                s.wait_ge(sSD.h, b[sSD] + 16 * (gi - 3))
                            c0 = hf * 1024 + tc2 * 512
                            qdma(s, sHD, o_t[gi % 4][:, :], out[oc * P:(oc + 1) * P, c0:c0 + 512]
                                        )

                    @block.vector
                    def _(v, hf=hf, b=b):
                        for gi in range(32):
                            oc, tc2 = divmod(gi, 2)
                            par = gi % 4
                            c0 = hf * 1024 + tc2 * 512
                            v.wait_ge(sPE.h, b[sPE] + gi + 1)
                            v.tensor_tensor(tmp7[gi % 2][:, :], PB[par][:, :], sigB_sb[:, c0:c0 + 512],
                                            op=ALU.mult).then_inc(sDV.h, 1)
                            v.wait_ge(sHD.h, b[sHD] + 16 * (gi + 1))
                            v.tensor_tensor(o_t[gi % 4][:, :], o_t[gi % 4][:, :], tmp7[gi % 2][:, :],
                                            op=ALU.add).then_inc(sDV.h, 1)
                    sDV.n += 64

                    @block.scalar
                    def _(sc, hf=hf, b=b):
                        for gi in range(32):
                            oc, tc2 = divmod(gi, 2)
                            c0 = hf * 1024 + tc2 * 512
                            sc.wait_ge(sDV.h, b[sDV] + 2 * gi + 2)
                            qdma(sc, sSD, out[oc * P:(oc + 1) * P, c0:c0 + 512], o_t[gi % 4][:, :]
                                         )

        s67.close()

        if PHASES >= 8:
            s8 = contextlib.ExitStack()
            with s8:
                comb = s8.enter_context(nc.sbuf_tensor("comb", [P, FC, 1024], F32))
                mid = s8.enter_context(nc.sbuf_tensor("mid", [P, 11, 1024], BF16))
                w13b = [s8.enter_context(nc.sbuf_tensor(f"w13b{i}", [P, FC, 256], BF16)) for i in range(2)]
                w2b = [s8.enter_context(nc.sbuf_tensor(f"w2b{i}", [P, 11, P], BF16)) for i in range(2)]
                wB = s8.enter_context(nc.sbuf_tensor("wB", [P, 1024], F32))
                wrow = s8.enter_context(nc.sbuf_tensor("wrow", [1, 1024], F32))
                sact8 = [s8.enter_context(nc.sbuf_tensor(f"sact8{i}", [P, 512], F32, side="right")) for i in range(2)]
                uw8 = [s8.enter_context(nc.sbuf_tensor(f"uw8{i}", [P, 512], F32, side="right")) for i in range(2)]
                o_t8 = [s8.enter_context(nc.sbuf_tensor(f"o_t8{i}", [P, 512], F32)) for i in range(4)]
                for hf in range(2):
                    bh = snap()

                    @block.vector
                    def _(v, hf=hf, bh=bh):
                        wait_all(v, [sSD, sPE], bh)
                        v.memset(comb[:, :, :], 0.0).then_inc(sDV.h, 1)
                    sDV.n += 1

                    for e in range(NEXP):
                        b = snap()
                        @block.gpsimd
                        def _(g, e=e, hf=hf, b=b):
                            if PHASES >= 5:
                                g.wait_ge(sDV.h, b[sDV])
                            qdma(g, sQD, wrow[:, 0:1024], w8t_sb[e:e + 1, hf * 1024:(hf + 1) * 1024]
                                        )

                        @block.tensor
                        def _(te, e=e, b=b):
                            te.wait_ge(sQD.h, b[sQD] + 16)
                            if e >= 1 or True:
                                te.wait_ge(sDV.h, b[sDV])
                            te.matmul(PB[6][:, :], ones1_f[:, :], wrow[:, 0:512], start=True, stop=True)
                            mm = te.matmul(PB[7][:, :], ones1_f[:, :], wrow[:, 512:1024], start=True, stop=True)
                            mm.then_inc(sPE.h, 1)
                        sPE.n += 1

                        @block.vector
                        def _(v, e=e, b=b):
                            v.wait_ge(sPE.h, b[sPE] + 1)
                            v.tensor_copy(wB[:, 0:512], PB[6][:, :])
                            v.tensor_copy(wB[:, 512:1024], PB[7][:, :]).then_inc(sDV.h, 1)
                        sDV.n += 1

                        b = snap()

                        @block.gpsimd
                        def _(g, e=e, b=b):
                            for mc in range(11):
                                if mc >= 2:
                                    g.wait_ge(sPE.h, b[sPE] + 2 * (mc - 1))
                                src1 = w13_t[e, :, mc * 256:(mc + 1) * 256].rearrange("(c p) n -> p c n", p=P)
                                qdma(g, sQD, w13b[mc % 2][:, :, :], src1)

                        @block.tensor
                        def _(te, e=e, hf=hf, b=b):
                            for mc in range(11):
                                te.wait_ge(sQD.h, b[sQD] + 16 * (mc + 1))
                                for tc in range(2):
                                    gi = mc * 2 + tc
                                    par = gi % 2
                                    if gi >= 2:
                                        te.wait_ge(sAC.h, b[sAC] + gi - 1)
                                        te.wait_ge(sDV.h, b[sDV] + 2 * (gi - 1))
                                    c0 = hf * 1024 + tc * 512
                                    for fc in range(FC):
                                        te.matmul(PB[0 + par][:, :], w13b[mc % 2][:, fc, 0:P],
                                                  xn2[:, fc, c0:c0 + 512],
                                                  start=(fc == 0), stop=(fc == FC - 1))
                                    for fc in range(FC):
                                        mm = te.matmul(PB[2 + par][:, :], w13b[mc % 2][:, fc, P:256],
                                                       xn2[:, fc, c0:c0 + 512],
                                                       start=(fc == 0), stop=(fc == FC - 1))
                                    mm.then_inc(sPE.h, 1)
                        sPE.n += 22

                        @block.scalar
                        def _(sc, e=e, b=b):
                            for gi in range(22):
                                par = gi % 2
                                sc.wait_ge(sPE.h, b[sPE] + gi + 1)
                                if gi >= 2:
                                    sc.wait_ge(sDV.h, b[sDV] + 2 * gi - 2)
                                sc.activation(sact8[par][:, :], PB[0 + par][:, :], AF.Silu).then_inc(sAC.h, 1)
                        sAC.n += 22

                        @block.vector
                        def _(v, e=e, hf=hf, b=b):
                            for gi in range(22):
                                mc, tc = divmod(gi, 2)
                                par = gi % 2
                                v.wait_ge(sAC.h, b[sAC] + gi + 1)
                                v.tensor_tensor(uw8[par][:, :], PB[2 + par][:, :],
                                                wB[:, tc * 512:(tc + 1) * 512], op=ALU.mult)
                                v.tensor_tensor(mid[:, mc, tc * 512:(tc + 1) * 512], uw8[par][:, :],
                                                sact8[par][:, :], op=ALU.mult).then_inc(sDV.h, 2)
                        sDV.n += 44

                        b = snap()

                        @block.gpsimd
                        def _(g, e=e, b=b):
                            for oc in range(16):
                                if oc >= 2:
                                    g.wait_ge(sPE.h, b[sPE] + 2 * (oc - 1))
                                src = w2_t[e, :, oc * P:(oc + 1) * P].rearrange("(c p) n -> p c n", p=P)
                                qdma(g, sQD, w2b[oc % 2][:, :, :], src)

                        @block.tensor
                        def _(te, e=e, b=b):
                            te.wait_ge(sDV.h, b[sDV])
                            for oc in range(16):
                                te.wait_ge(sQD.h, b[sQD] + 16 * (oc + 1))
                                for tc in range(2):
                                    gi = oc * 2 + tc
                                    par = 4 + gi % 2
                                    if gi >= 2:
                                        te.wait_ge(sDV.h, b[sDV] + gi - 1)
                                    for mc in range(11):
                                        mm = te.matmul(PB[par][:, :], w2b[oc % 2][:, mc, :],
                                                       mid[:, mc, tc * 512:(tc + 1) * 512],
                                                       start=(mc == 0), stop=(mc == 10))
                                    mm.then_inc(sPE.h, 1)
                        sPE.n += 32

                        @block.vector
                        def _(v, e=e, b=b):
                            for gi in range(32):
                                oc, tc = divmod(gi, 2)
                                par = 4 + gi % 2
                                v.wait_ge(sPE.h, b[sPE] + gi + 1)
                                v.tensor_tensor(comb[:, oc, tc * 512:(tc + 1) * 512],
                                                comb[:, oc, tc * 512:(tc + 1) * 512],
                                                PB[par][:, :], op=ALU.add).then_inc(sDV.h, 1)
                        sDV.n += 32

                    b = snap()

                    @block.sync
                    def _(s, hf=hf, b=b):
                        wait_all(s, [sSD, sDV], b)
                        for gi in range(32):
                            oc, tc = divmod(gi, 2)
                            if gi >= 4:
                                s.wait_ge(sSD.h, b[sSD] + 16 * (gi - 3))
                            c0 = hf * 1024 + tc * 512
                            qdma(s, sHD, o_t8[gi % 4][:, :], out[oc * P:(oc + 1) * P, c0:c0 + 512]
                                        )

                    @block.vector
                    def _(v, hf=hf, b=b):
                        for gi in range(32):
                            oc, tc = divmod(gi, 2)
                            v.wait_ge(sHD.h, b[sHD] + 16 * (gi + 1))
                            v.tensor_tensor(o_t8[gi % 4][:, :], o_t8[gi % 4][:, :],
                                            comb[:, oc, tc * 512:(tc + 1) * 512], op=ALU.add).then_inc(sDV.h, 1)
                    sDV.n += 32

                    @block.scalar
                    def _(sc, hf=hf, b=b):
                        for gi in range(32):
                            oc, tc = divmod(gi, 2)
                            c0 = hf * 1024 + tc * 512
                            sc.wait_ge(sDV.h, b[sDV] + gi + 1)
                            qdma(sc, sSD, out[oc * P:(oc + 1) * P, c0:c0 + 512], o_t8[gi % 4][:, :]
                                         )

        if KDBG == "k":
            fin0 = snap()

            @block.gpsimd
            def _(g):
                wait_all(g, ALL, fin0)
                qdma(g, sQD, out[0:1024, :].rearrange("(c p) t -> p c t", p=P),
                     k_sb[:, :, 64:64 + T])

        if KDBG == "v":
            fin0 = snap()

            @block.gpsimd
            def _(g):
                wait_all(g, ALL, fin0)
                qdma(g, sQD, dbg.rearrange("(tt p) o -> p tt o", p=P), vT_sb[:, :, :])

        if KDBG == "rt":
            fin0 = snap()

            @block.gpsimd
            def _(g):
                wait_all(g, ALL, fin0)
                qdma(g, sQD, dbg[128:256, 0:32], rdbg[:, :])

        if KDBG == "w8":
            fin0 = snap()

            @block.gpsimd
            def _(g):
                wait_all(g, ALL, fin0)
                qdma(g, sQD, dbg[0:8, 0:1024], w8t_sb[0:8, 0:1024])
                qdma(g, sQD, dbg[8:16, 0:1024], w8t_sb[0:8, 1024:2048])

        if KDBG == "xn1":
            fin0 = snap()

            @block.gpsimd
            def _(g):
                wait_all(g, ALL, fin0)
                qdma(g, sQD, out.rearrange("(c p) t -> p c t", p=P),
                            xn1[:, :, 64:64 + T])
        if KDBG == "ao":
            fin0 = snap()

            @block.gpsimd
            def _(g):
                wait_all(g, ALL, fin0)
                qdma(g, sQD, out, ao_scr)

        fin = snap()

        @block.sync
        def _(s):
            s.wait_ge(sSD.h, fin[sSD])
            s.wait_ge(sQD.h, fin[sQD])

        @block.gpsimd
        def _(g):
            g.wait_ge(sSD.h, fin[sSD])

    return nc


def _masks():
    r = np.arange(P)[None, :]
    cb = np.arange(P)[:, None]
    mm = np.zeros((P, 2, P), np.float32)
    mm[:, 0, :] = ((cb >= r + 57) & (cb <= r + 64)).astype(np.float32)
    cbB = (128 + np.arange(P))[:, None]
    bm = ((cbB >= r + 57) & (cbB <= r + 64)).astype(np.float32)
    mm[0:64, 1, :] = bm[0:64]
    mmi = np.ones((4, P), np.float32)

    mf = np.zeros((P, 2, P), np.float32)
    j = cb - 64
    mf[:, 0, :] = ((j >= 4) & (j <= r) & (r - j < 8)).astype(np.float32)
    jB = cbB - 64
    bf = ((jB >= 4) & (jB <= r) & (r - jB < 8)).astype(np.float32)
    mf[0:64, 1, :] = bf[0:64]
    jj = np.arange(4)[:, None]
    mfi = (jj <= r).astype(np.float32)
    return (np.ascontiguousarray(mf.reshape(P, 256)),
            np.ascontiguousarray(mm.reshape(P, 256)),
            np.ascontiguousarray(mfi), np.ascontiguousarray(mmi))


def kernel(L0_attn_x, L0_attn_Wq, L0_attn_Wk, L0_attn_Wv, L0_attn_Wo,
           L0_ffn_gate_w, L0_ffn_shared_gate_w, L0_ffn_shared_gate_up_w,
           L0_ffn_shared_down_w, L0_ffn_w13, L0_ffn_w2):
    x = np.asarray(L0_attn_x, dtype=np.float32)
    cw = lambda a: np.ascontiguousarray(np.asarray(a, dtype=np.float32))
    wq_t = cw(np.asarray(L0_attn_Wq).T)
    wk_t = cw(np.asarray(L0_attn_Wk).T)
    wv_t = cw(np.asarray(L0_attn_Wv).T)
    wo_t = cw(np.asarray(L0_attn_Wo).T)
    gate_t = cw(np.asarray(L0_ffn_gate_w).T)
    sg_t = cw(np.asarray(L0_ffn_shared_gate_w).T)
    gu_t = np.asarray(L0_ffn_shared_gate_up_w, dtype=np.float32).T
    gu_t = cw(gu_t.reshape(2048, 2, 44, 128).transpose(0, 2, 1, 3).reshape(2048, 11264))
    dn_t = cw(np.asarray(L0_ffn_shared_down_w).T)
    w13 = np.asarray(L0_ffn_w13, dtype=np.float32)
    w2 = np.asarray(L0_ffn_w2, dtype=np.float32)
    w13_t = np.transpose(w13, (0, 2, 1)).astype(np.float32)
    w13_t = cw(w13_t.reshape(8, 2048, 2, 11, 128).transpose(0, 1, 3, 2, 4).reshape(8, 2048, 2816))
    w2_t = cw(np.transpose(w2, (0, 2, 1)))
    gate_pt = cw(gate_t.reshape(FC, P, 8).transpose(1, 0, 2))
    sg_pt = cw(sg_t.reshape(FC, P, 1).transpose(1, 0, 2))
    eye = np.eye(P, dtype=np.float32)
    mF, mM, mFI, mMI = _masks()

    nc = build_nc()

    in_maps = []
    for c in range(8):
        bb, si = divmod(c, 4)
        s0 = si * T
        xf = x[bb].T
        pre = xf[:, s0 - PRE:s0] if s0 > 0 else np.zeros((D, PRE), np.float32)
        slab = np.concatenate([pre, xf[:, s0:s0 + T], xf[:, 0:4],
                               np.zeros((D, W - T - PRE - 4), np.float32)], axis=1)
        in_maps.append(dict(
            x_slab=np.ascontiguousarray(slab), wq_t=wq_t, wk_t=wk_t, wv_t=wv_t, wo_t=wo_t,
            gate_pt=gate_pt, sg_pt=sg_pt, gu_t=gu_t, dn_t=dn_t, w13_t=w13_t, w2_t=w2_t,
            maskF=(mF if s0 == 0 else mM), maskM=mM,
            maskFI=(mFI if s0 == 0 else mMI), maskMI=mMI, eye_in=eye))

    res = run_bass_kernel_spmd(nc, in_maps, core_ids=list(range(8)))
    if KDBG:
        global _DBG_RES
        _DBG_RES = res

    full = np.empty((2, 8192, D), np.float32)
    for c in range(8):
        bb, si = divmod(c, 4)
        s0 = si * T
        full[bb, s0:s0 + T, :] = res.results[c]["out"].T
    return full


# version 09
# speedup vs baseline: 1.1531x; 1.0269x over previous
import contextlib
import numpy as np
import concourse.bass as bass
import concourse.mybir as mybir
from concourse.bass_utils import run_bass_kernel_spmd

F32 = mybir.dt.float32
BF16 = mybir.dt.bfloat16
AF = mybir.ActivationFunctionType
ALU = mybir.AluOpType
AX = mybir.AxisListType

P = 128
FC = 16
D = 2048
T = 2048
PRE = 64
W = 2176
ICOL = 2112
QSCALE = float(1.0 / np.sqrt(128.0))
SHIFT = -20.0
INVD = float(1.0 / 2048.0)
EPS = 1e-5
NEXP = 8
MOE_I = 1408
SH_I = 5632
CC_ALL = [(0, 512), (512, 512), (1024, 512), (1536, 512), (2048, 128)]
CC_MAIN = [(0, 512), (512, 512), (1024, 512), (1536, 512)]

import os
PHASES = int(os.environ.get("KPH", "99"))
KDBG = os.environ.get("KDBG", "")
KBLK = int(os.environ.get("KBLK", "4"))
KSKIP = os.environ.get("KSKIP", "")
KSEC = os.environ.get("KSEC", "abc")
KP6 = os.environ.get("KP6", "sg,gu")
KNJ = int(os.environ.get("KNJ", "44"))


class SemT:
    def __init__(self, nc, st, name):
        self.h = st.enter_context(nc.semaphore(name))
        self.n = 0

    def inc(self, instr, k=1):
        instr.then_inc(self.h, k)
        self.n += k
        return self.n


def wait_all(eng, sems, snap):
    for s in sems:
        if snap[s] > 0:
            eng.wait_ge(s.h, snap[s])


def build_nc():
    nc = bass.Bass("TRN2", target_bir_lowering=False)

    x_slab = nc.dram_tensor("x_slab", [D, W], F32, kind="ExternalInput").ap()
    wq_t = nc.dram_tensor("wq_t", [D, D], F32, kind="ExternalInput").ap()
    wk_t = nc.dram_tensor("wk_t", [D, 1024], F32, kind="ExternalInput").ap()
    wv_t = nc.dram_tensor("wv_t", [D, 1024], F32, kind="ExternalInput").ap()
    wo_t = nc.dram_tensor("wo_t", [D, D], F32, kind="ExternalInput").ap()
    gate_pt = nc.dram_tensor("gate_pt", [P, FC, 8], F32, kind="ExternalInput").ap()
    sg_pt = nc.dram_tensor("sg_pt", [P, FC, 1], F32, kind="ExternalInput").ap()
    gu_t = nc.dram_tensor("gu_t", [D, 2 * SH_I], F32, kind="ExternalInput").ap()
    dn_t = nc.dram_tensor("dn_t", [SH_I, D], F32, kind="ExternalInput").ap()
    w13_t = nc.dram_tensor("w13_t", [NEXP, D, 2 * MOE_I], F32, kind="ExternalInput").ap()
    w2_t = nc.dram_tensor("w2_t", [NEXP, MOE_I, D], F32, kind="ExternalInput").ap()
    maskF = nc.dram_tensor("maskF", [P, 256], F32, kind="ExternalInput").ap()
    maskM = nc.dram_tensor("maskM", [P, 256], F32, kind="ExternalInput").ap()
    maskFI = nc.dram_tensor("maskFI", [4, P], F32, kind="ExternalInput").ap()
    maskMI = nc.dram_tensor("maskMI", [4, P], F32, kind="ExternalInput").ap()
    eye_in = nc.dram_tensor("eye_in", [P, P], F32, kind="ExternalInput").ap()

    out = nc.dram_tensor("out", [D, T], F32, kind="ExternalOutput").ap()
    dbg = nc.dram_tensor("dbg", [17 * P, 1024], F32, kind="ExternalOutput").ap()
    ao_scr = nc.dram_tensor("ao_scr", [D, T], BF16, kind="Internal").ap()
    hs_scr = nc.dram_tensor("hs_scr", [SH_I, T], BF16, kind="Internal").ap()

    st = contextlib.ExitStack()
    with st:
        sQD = SemT(nc, st, "sQD")
        sHD = SemT(nc, st, "sHD")
        sSD = SemT(nc, st, "sSD")
        sPE = SemT(nc, st, "sPE")
        sDV = SemT(nc, st, "sDV")
        sAC = SemT(nc, st, "sAC")
        ALL = [sQD, sHD, sSD, sPE, sDV, sAC]

        def snap():
            return {s: s.n for s in ALL}

        def qdma(eng, sem, dst, srcap):
            if sem.n > 0:
                eng.wait_ge(sem.h, sem.n)
            eng.dma_start(dst, srcap).then_inc(sem.h, 16)
            sem.n += 16

        PB = [st.enter_context(nc.psum_tensor(f"PB{i}", [P, 512], F32)) for i in range(8)]

        ones_bf = st.enter_context(nc.sbuf_tensor("ones_bf", [P, 1], BF16))
        ones1_f = st.enter_context(nc.sbuf_tensor("ones1_f", [1, P], F32))
        maskF_sb = st.enter_context(nc.sbuf_tensor("maskF_sb", [P, 2, P], BF16))
        maskM_sb = st.enter_context(nc.sbuf_tensor("maskM_sb", [P, 2, P], BF16))
        maskFI_sb = st.enter_context(nc.sbuf_tensor("maskFI_sb", [4, P], BF16))
        maskMI_sb = st.enter_context(nc.sbuf_tensor("maskMI_sb", [4, P], BF16))
        eye_sb = st.enter_context(nc.sbuf_tensor("eye_sb", [P, P], F32))
        gate_sb = st.enter_context(nc.sbuf_tensor("gate_sb", [P, FC, 8], F32))
        sgw_sb = st.enter_context(nc.sbuf_tensor("sgw_sb", [P, FC, 1], BF16))
        shift_sb = st.enter_context(nc.sbuf_tensor("shift_sb", [P, 1], F32))

        block = st.enter_context(nc.Block())

        @block.gpsimd
        def _(g):
            qdma(g, sQD, maskF_sb[:, :, :], maskF)
            qdma(g, sQD, maskM_sb[:, :, :], maskM)
            qdma(g, sQD, maskFI_sb[:, :], maskFI)
            qdma(g, sQD, maskMI_sb[:, :], maskMI)
            qdma(g, sQD, eye_sb[:, :], eye_in)
            qdma(g, sQD, gate_sb[:, :, :], gate_pt)
            qdma(g, sQD, sgw_sb[:, :, :], sg_pt)

        @block.vector
        def _(v):
            v.memset(ones_bf[:, :], 1.0)
            v.memset(shift_sb[:, :], SHIFT)
            v.memset(ones1_f[:, :], 1.0).then_inc(sDV.h, 1)
        sDV.n += 1

        s13 = st.enter_context(contextlib.ExitStack())
        xn1 = s13.enter_context(nc.sbuf_tensor("xn1", [P, FC, W], BF16))

        if PHASES >= 1:
            s1 = contextlib.ExitStack()
            with s1:
                xin = [s1.enter_context(nc.sbuf_tensor(f"xin{i}", [P, FC, 512], F32)) for i in range(2)]
                xbf = s1.enter_context(nc.sbuf_tensor("xbf", [P, FC, 512], BF16))
                xsq = s1.enter_context(nc.sbuf_tensor("xsq", [P, FC, 512], BF16))
                t1 = s1.enter_context(nc.sbuf_tensor("t1", [P, 512], F32))
                rows = [s1.enter_context(nc.sbuf_tensor(f"r1_{k}", [1, 2, 512], F32))
                        for k in range(6)]
                Rm, Rq, Rmsq, Rsd, Rr, Rmr = rows
                b = snap()
                NC = len(CC_ALL)

                @block.sync
                def _(s):
                    for i, (c0, wn) in enumerate(CC_ALL):
                        if i >= 2:
                            s.wait_ge(sDV.h, b[sDV] + 3 * (i - 2) + 3)
                        src = x_slab[:, c0:c0 + wn].rearrange("(c p) t -> p c t", p=P)
                        qdma(s, sHD, xin[i % 2][:, :, 0:wn], src)

                @block.gpsimd
                def _(g):
                    for i, (c0, wn) in enumerate(CC_ALL):
                        if i >= 1:
                            g.wait_ge(sPE.h, b[sPE] + 2 * (i - 1) + 1)
                        src = x_slab[:, c0:c0 + wn].rearrange("(c p) t -> p c t", p=P)
                        qdma(g, sQD, xbf[:, :, 0:wn], src)

                @block.scalar
                def _(sc):
                    for i, (c0, wn) in enumerate(CC_ALL):
                        sc.wait_ge(sHD.h, b[sHD] + 16 * (i + 1))
                        if i >= 1:
                            sc.wait_ge(sPE.h, b[sPE] + 2 * (i - 1) + 1)
                        sc.activation(xsq[:, :, 0:wn], xin[i % 2][:, :, 0:wn], AF.Square).then_inc(sAC.h, 1)
                        sc.wait_ge(sDV.h, b[sDV] + 3 * i + 1)
                        sc.activation(Rsd[:, i % 2, 0:wn], Rq[:, i % 2, 0:wn], AF.Sqrt).then_inc(sAC.h, 1)
                sAC.n += 2 * NC

                @block.tensor
                def _(te):
                    te.wait_ge(sDV.h, b[sDV])
                    for i, (c0, wn) in enumerate(CC_ALL):
                        te.wait_ge(sQD.h, b[sQD] + 16 * (i + 1))
                        te.wait_ge(sAC.h, b[sAC] + 2 * i + 1)
                        if i >= 1:
                            te.wait_ge(sDV.h, b[sDV] + 3 * (i - 1) + 1)
                        for fc in range(FC):
                            te.matmul(PB[0][0:1, 0:wn], ones_bf[:, :], xbf[:, fc, 0:wn],
                                      start=(fc == 0), stop=(fc == FC - 1))
                        for fc in range(FC):
                            mm = te.matmul(PB[1][0:1, 0:wn], ones_bf[:, :], xsq[:, fc, 0:wn],
                                           start=(fc == 0), stop=(fc == FC - 1))
                        mm.then_inc(sPE.h, 1)
                        te.wait_ge(sDV.h, b[sDV] + 3 * i + 2)
                        if i >= 2:
                            te.wait_ge(sDV.h, b[sDV] + 3 * (i - 2) + 3)
                        te.matmul(PB[2 + i % 2][:, 0:wn], ones1_f[:, :], Rr[:, i % 2, 0:wn],
                                  start=True, stop=True)
                        mm = te.matmul(PB[4 + i % 2][:, 0:wn], ones1_f[:, :], Rmr[:, i % 2, 0:wn],
                                       start=True, stop=True)
                        mm.then_inc(sPE.h, 1)
                sPE.n += 2 * NC

                @block.vector
                def _(v):
                    for i, (c0, wn) in enumerate(CC_ALL):
                        ip = i % 2
                        v.wait_ge(sPE.h, b[sPE] + 2 * i + 1)
                        if i >= 2:
                            v.wait_ge(sAC.h, b[sAC] + 2 * (i - 2) + 2)
                            v.wait_ge(sPE.h, b[sPE] + 2 * (i - 2) + 2)
                        v.tensor_scalar(Rm[:, ip, 0:wn], PB[0][0:1, 0:wn], INVD, None, op0=ALU.mult)
                        v.tensor_scalar(Rq[:, ip, 0:wn], PB[1][0:1, 0:wn], INVD, None, op0=ALU.mult)
                        v.tensor_tensor(Rmsq[:, ip, 0:wn], Rm[:, ip, 0:wn], Rm[:, ip, 0:wn], op=ALU.mult)
                        v.tensor_tensor(Rq[:, ip, 0:wn], Rq[:, ip, 0:wn], Rmsq[:, ip, 0:wn],
                                        op=ALU.subtract)
                        v.tensor_scalar(Rq[:, ip, 0:wn], Rq[:, ip, 0:wn], EPS, None,
                                        op0=ALU.add).then_inc(sDV.h, 1)
                        v.wait_ge(sAC.h, b[sAC] + 2 * i + 2)
                        v.reciprocal(Rr[:, ip, 0:wn], Rsd[:, ip, 0:wn])
                        v.tensor_tensor(Rmr[:, ip, 0:wn], Rm[:, ip, 0:wn], Rr[:, ip, 0:wn],
                                        op=ALU.mult).then_inc(sDV.h, 1)
                        v.wait_ge(sPE.h, b[sPE] + 2 * i + 2)
                        for fc in range(FC):
                            v.tensor_tensor(t1[:, 0:wn], xin[ip][:, fc, 0:wn], PB[2 + ip][:, 0:wn], op=ALU.mult)
                            tt = v.tensor_tensor(xn1[:, fc, c0:c0 + wn], t1[:, 0:wn], PB[4 + ip][:, 0:wn],
                                                 op=ALU.subtract)
                        tt.then_inc(sDV.h, 1)
                sDV.n += 3 * NC

        if PHASES >= 2:
            k_sb = s13.enter_context(nc.sbuf_tensor("k_sb", [P, 8, W], BF16))
            vT_sb = s13.enter_context(nc.sbuf_tensor("vT_sb", [P, 17, 1024], BF16))
            vI_sb = s13.enter_context(nc.sbuf_tensor("vI_sb", [4, 1024], BF16))
            s2 = contextlib.ExitStack()
            with s2:
                wkbuf = [s2.enter_context(nc.sbuf_tensor(f"wkbuf{i}", [P, FC, P], BF16)) for i in range(2)]
                wvbuf = [s2.enter_context(nc.sbuf_tensor(f"wvbuf{i}", [P, FC, 512], BF16)) for i in range(2)]
                b = snap()
                NCC = len(CC_ALL)

                @block.gpsimd
                def _(g):
                    for ock in range(8):
                        if ock >= 2:
                            g.wait_ge(sPE.h, b[sPE] + NCC * (ock - 1))
                        else:
                            g.wait_ge(sPE.h, b[sPE])
                        src = wk_t[:, ock * P:(ock + 1) * P].rearrange("(c p) n -> p c n", p=P)
                        qdma(g, sQD, wkbuf[ock % 2][:, :, :], src)
                    for on in range(2):
                        if on >= 2:
                            g.wait_ge(sPE.h, b[sPE] + NCC * 8 + 18 * (on - 1))
                        src = wv_t[:, on * 512:(on + 1) * 512].rearrange("(c p) n -> p c n", p=P)
                        qdma(g, sQD, wvbuf[on % 2][:, :, :], src)

                @block.tensor
                def _(te):
                    wait_all(te, [sDV, sQD], b)
                    gi = 0
                    for ock in range(8):
                        te.wait_ge(sQD.h, b[sQD] + 16 * (ock + 1))
                        for cc, (c0, wn) in enumerate(CC_ALL):
                            par = gi % 2
                            if gi >= 2:
                                te.wait_ge(sDV.h, b[sDV] + gi - 1)
                            for fc in range(FC):
                                mm = te.matmul(PB[par][:, 0:wn], wkbuf[ock % 2][:, fc, :],
                                               xn1[:, fc, c0:c0 + wn],
                                               start=(fc == 0), stop=(fc == FC - 1))
                            mm.then_inc(sPE.h, 1)
                            gi += 1
                    for on in range(2):
                        te.wait_ge(sQD.h, b[sQD] + 16 * (9 + on))
                        for tt in range(17):
                            par = gi % 2
                            if gi >= 2:
                                te.wait_ge(sDV.h, b[sDV] + gi - 1)
                            for fc in range(FC):
                                mm = te.matmul(PB[par][:, :], xn1[:, fc, tt * P:(tt + 1) * P],
                                               wvbuf[on % 2][:, fc, :],
                                               start=(fc == 0), stop=(fc == FC - 1))
                            mm.then_inc(sPE.h, 1)
                            gi += 1
                        par = gi % 2
                        if gi >= 2:
                            te.wait_ge(sDV.h, b[sDV] + gi - 1)
                        for fc in range(FC):
                            mm = te.matmul(PB[par][0:4, :], xn1[:, fc, ICOL:ICOL + 4],
                                           wvbuf[on % 2][:, fc, :],
                                           start=(fc == 0), stop=(fc == FC - 1))
                        mm.then_inc(sPE.h, 1)
                        gi += 1
                sPE.n += 8 * NCC + 2 * 18

                @block.vector
                def _(v):
                    gi = 0
                    for ock in range(8):
                        for cc, (c0, wn) in enumerate(CC_ALL):
                            v.wait_ge(sPE.h, b[sPE] + gi + 1)
                            v.tensor_copy(k_sb[:, ock, c0:c0 + wn], PB[gi % 2][:, 0:wn]).then_inc(sDV.h, 1)
                            gi += 1
                    for on in range(2):
                        for tt in range(17):
                            v.wait_ge(sPE.h, b[sPE] + gi + 1)
                            v.tensor_copy(vT_sb[:, tt, on * 512:(on + 1) * 512],
                                          PB[gi % 2][:, :]).then_inc(sDV.h, 1)
                            gi += 1
                        v.wait_ge(sPE.h, b[sPE] + gi + 1)
                        v.tensor_copy(vI_sb[0:4, on * 512:(on + 1) * 512],
                                      PB[gi % 2][0:4, :]).then_inc(sDV.h, 1)
                        gi += 1
                sDV.n += 8 * NCC + 2 * 18

        if PHASES >= 3:
            s3 = contextlib.ExitStack()
            with s3:
                wqg = [s3.enter_context(nc.sbuf_tensor(f"wqg{i}", [P, FC, 256], BF16)) for i in range(2)]
                q_blk = s3.enter_context(nc.sbuf_tensor("q_blk", [P, FC, 512], BF16))
                av_blk = s3.enter_context(nc.sbuf_tensor("av_blk", [P, FC, 512], BF16))
                pt_sb = [s3.enter_context(nc.sbuf_tensor(f"pt{i}", [P, 2, P], BF16)) for i in range(3)]
                ptI_sb = [s3.enter_context(nc.sbuf_tensor(f"ptI{i}", [4, P], BF16)) for i in range(3)]
                rden = [s3.enter_context(nc.sbuf_tensor(f"rden{i}", [1, P], F32)) for i in range(3)]
                denB = [s3.enter_context(nc.sbuf_tensor(f"denB{i}", [P, P], F32)) for i in range(3)]
                ao_st = [s3.enter_context(nc.sbuf_tensor(f"ao_st{i}", [P, 512], BF16)) for i in range(2)]
                dummy = s3.enter_context(nc.sbuf_tensor("dmy_sb", [P, 512], F32))
                dmy = s3.enter_context(nc.sbuf_tensor("dmy_pt", [P, 2, P], BF16))
                dmy_wq = s3.enter_context(nc.sbuf_tensor("dmy_wq", [P, FC, 256], BF16))
                dmyI = s3.enter_context(nc.sbuf_tensor("dmy_ptI", [4, P], BF16))

                SCB = [2, 3, 6]
                AVB = [4, 5, 7]

                @block.vector
                def _(v):
                    v.memset(pt_sb[0][:, :, :], 0.0)
                    v.memset(pt_sb[1][:, :, :], 0.0)
                    v.memset(pt_sb[2][:, :, :], 0.0).then_inc(sDV.h, 1)
                sDV.n += 1

                for bi in range(KBLK):
                    b = snap()

                    @block.gpsimd
                    def _(g, bi=bi, b=b):
                        for ocg in range(8):
                            if ocg >= 2:
                                g.wait_ge(sPE.h, b[sPE] + 2 * (ocg - 1))
                            else:
                                g.wait_ge(sPE.h, b[sPE])
                            src = wq_t[:, ocg * 256:(ocg + 1) * 256].rearrange("(c p) n -> p c n", p=P)
                            if KSKIP == "qdma":
                                qdma(g, sQD, dmy_wq[:, :, :], src)
                            else:
                                qdma(g, sQD, wqg[ocg % 2][:, :, :], src)

                    @block.tensor
                    def _(te, bi=bi, b=b):
                        te.wait_ge(sDV.h, b[sDV])
                        bc0 = 64 + bi * 512
                        for ocg in range(8):
                            te.wait_ge(sQD.h, b[sQD] + 16 * (ocg + 1))
                            for oi in range(2):
                                oc = ocg * 2 + oi
                                par = oc % 2
                                if oc >= 2:
                                    te.wait_ge(sDV.h, b[sDV] + oc - 1)
                                if KSKIP == "qmm":
                                    mm = te.matmul(PB[par][:, :], dmy_wq[:, 0, 0:P],
                                                   xn1[:, 0, bc0:bc0 + 512], start=True, stop=True)
                                else:
                                    for fc in range(FC):
                                        mm = te.matmul(PB[par][:, :], wqg[ocg % 2][:, fc, oi * P:(oi + 1) * P],
                                                       xn1[:, fc, bc0:bc0 + 512],
                                                       start=(fc == 0), stop=(fc == FC - 1))
                                mm.then_inc(sPE.h, 1)
                    sPE.n += 16

                    @block.vector
                    def _(v, bi=bi, b=b):
                        for oc in range(16):
                            v.wait_ge(sPE.h, b[sPE] + oc + 1)
                            _qdst = dummy[:, :].bitcast(BF16)[:, 0:512] if KSKIP == "qevac" else q_blk[:, oc, :]
                            v.tensor_scalar(_qdst, PB[oc % 2][:, :], QSCALE, None,
                                            op0=ALU.mult).then_inc(sDV.h, 1)
                    sDV.n += 16

                    if "b" not in KSEC:
                        continue
                    b = snap()
                    NU = 64
                    pe_scores = [0] * NU
                    pe_den = [0] * NU
                    pe_av = [0] * NU
                    pe_denB = [0] * NU
                    c = b[sPE]
                    for u in range(NU + 1):
                        if u < NU:
                            c += 1
                            pe_scores[u] = c
                        if u >= 1:
                            c += 1
                            pe_den[u - 1] = c
                            c += 1
                            pe_av[u - 1] = c
                            c += 1
                            pe_denB[u - 1] = c
                    pe_total = c
                    dv_mask = [0] * NU
                    dv_rden = [0] * NU
                    dv_denBe = [0] * NU
                    dv_stt = [0] * NU
                    c = b[sDV]
                    for u in range(NU):
                        c += 1
                        dv_mask[u] = c
                        c += 1
                        dv_rden[u] = c
                        c += 1
                        dv_denBe[u] = c
                        c += 1
                        dv_stt[u] = c
                    dv_total = c
                    ac_exp = [b[sAC] + u + 1 for u in range(NU)]

                    first_mask = (bi == 0)

                    @block.tensor
                    def _(te, bi=bi, b=b, pe_scores=pe_scores, dv_mask=dv_mask,
                          dv_rden=dv_rden, dv_denBe=dv_denBe, dv_stt=dv_stt, ac_exp=ac_exp):
                        te.wait_ge(sDV.h, b[sDV])
                        for u in range(NU + 1):
                            if u < NU:
                                qtl, h = divmod(u, 16)
                                qt = bi * 4 + qtl
                                par = u % 3
                                kvc = h // 2
                                kc0 = qt * P
                                if u >= 3:
                                    te.wait_ge(sAC.h, ac_exp[u - 3])
                                    te.wait_ge(sDV.h, dv_denBe[u - 3])
                                te.matmul(PB[SCB[par]][:, 0:128], k_sb[:, kvc, kc0:kc0 + P],
                                          q_blk[:, h, qtl * P:(qtl + 1) * P], start=True, stop=True)
                                te.matmul(PB[SCB[par]][0:64, 128:256], k_sb[:, kvc, kc0 + P:kc0 + P + 64],
                                          q_blk[:, h, qtl * P:(qtl + 1) * P], start=True, stop=True)
                                mm = te.matmul(PB[SCB[par]][0:4, 256:384], k_sb[:, kvc, ICOL:ICOL + 4],
                                               q_blk[:, h, qtl * P:(qtl + 1) * P], start=True, stop=True)
                                mm.then_inc(sPE.h, 1)
                            if u >= 1:
                                v = u - 1
                                qtl, h = divmod(v, 16)
                                qt = bi * 4 + qtl
                                par = v % 3
                                kvc = h // 2
                                te.wait_ge(sDV.h, dv_mask[v])
                                te.matmul(PB[SCB[par]][0:1, 384:512], ones_bf[:, :], pt_sb[par][:, 0, :],
                                          start=True, stop=False)
                                te.matmul(PB[SCB[par]][0:1, 384:512], ones_bf[:, :], pt_sb[par][:, 1, :],
                                          start=False, stop=False)
                                mm = te.matmul(PB[SCB[par]][0:1, 384:512], ones_bf[0:4, :], ptI_sb[par][0:4, :],
                                               start=False, stop=True)
                                mm.then_inc(sPE.h, 1)
                                if v >= 3:
                                    te.wait_ge(sDV.h, dv_stt[v - 3])
                                te.matmul(PB[AVB[par]][:, 0:128], vT_sb[:, qt, kvc * P:(kvc + 1) * P],
                                          pt_sb[par][:, 0, :], start=True, stop=False)
                                te.matmul(PB[AVB[par]][:, 0:128], vT_sb[0:64, qt + 1, kvc * P:(kvc + 1) * P],
                                          pt_sb[par][0:64, 1, :], start=False, stop=False)
                                mm = te.matmul(PB[AVB[par]][:, 0:128], vI_sb[0:4, kvc * P:(kvc + 1) * P],
                                               ptI_sb[par][0:4, :], start=False, stop=True)
                                mm.then_inc(sPE.h, 1)
                                te.wait_ge(sDV.h, dv_rden[v])
                                mm = te.matmul(PB[AVB[par]][:, 128:256], ones1_f[:, :], rden[par][:, :],
                                               start=True, stop=True)
                                mm.then_inc(sPE.h, 1)

                    @block.scalar
                    def _(sc, b=b, pe_scores=pe_scores, pe_av=pe_av):
                        for u in range(NU):
                            par = u % 3
                            sc.wait_ge(sPE.h, pe_scores[u])
                            if u >= 3:
                                sc.wait_ge(sPE.h, pe_av[u - 3])
                            if KSKIP == "exp":
                                _d1, _d2, _d3 = (dmy[:, 0, :], dmy[0:64, 1, :], dmyI[0:4, :])
                            else:
                                _d1, _d2, _d3 = (pt_sb[par][:, 0, :], pt_sb[par][0:64, 1, :], ptI_sb[par][0:4, :])
                            sc.activation(_d1, PB[SCB[par]][:, 0:128], AF.Exp, bias=shift_sb[:, :])
                            sc.activation(_d2, PB[SCB[par]][0:64, 128:256], AF.Exp, bias=shift_sb[0:64, :])
                            sc.activation(_d3, PB[SCB[par]][0:4, 256:384], AF.Exp,
                                          bias=shift_sb[0:4, :]).then_inc(sAC.h, 1)
                    sAC.n += NU

                    @block.vector
                    def _(v, bi=bi, b=b, ac_exp=ac_exp, pe_den=pe_den, pe_av=pe_av,
                          pe_denB=pe_denB, first_mask=first_mask):
                        for u in range(NU):
                            qtl, h = divmod(u, 16)
                            qt = bi * 4 + qtl
                            par = u % 3
                            msk = maskF_sb if (first_mask and qtl == 0) else maskM_sb
                            mski = maskFI_sb if (first_mask and qtl == 0) else maskMI_sb
                            v.wait_ge(sAC.h, ac_exp[u])
                            v.tensor_tensor(pt_sb[par][:, :, :], pt_sb[par][:, :, :], msk[:, :, :],
                                            op=ALU.mult)
                            v.tensor_tensor(ptI_sb[par][0:4, :], ptI_sb[par][0:4, :], mski[0:4, :],
                                            op=ALU.mult).then_inc(sDV.h, 1)
                            v.wait_ge(sPE.h, pe_den[u])
                            v.reciprocal(rden[par][:, :], PB[SCB[par]][0:1, 384:512]).then_inc(sDV.h, 1)
                            v.wait_ge(sPE.h, pe_denB[u])
                            _dbedst = dummy[:, 0:128] if KSKIP == "denbe" else denB[par][:, :]
                            v.tensor_copy(_dbedst, PB[AVB[par]][:, 128:256]).then_inc(sDV.h, 1)
                            v.wait_ge(sPE.h, pe_av[u])
                            _sttdst = dummy[:, 0:128].bitcast(BF16)[:, 0:128] if KSKIP == "stt" else av_blk[:, h, qtl * P:(qtl + 1) * P]
                            v.scalar_tensor_tensor(_sttdst, PB[AVB[par]][:, 0:128],
                                                   1.0, denB[par][:, :], op0=ALU.mult,
                                                   op1=ALU.mult).then_inc(sDV.h, 1)
                    sDV.n += 4 * NU
                    assert sPE.n + (4 * NU) == pe_total
                    sPE.n = pe_total
                    assert sDV.n == dv_total

                    if "c" not in KSEC:
                        continue
                    b = snap()

                    @block.gpsimd
                    def _(g, bi=bi, b=b):
                        for ocg in range(8):
                            if ocg >= 2:
                                g.wait_ge(sPE.h, b[sPE] + 2 * (ocg - 1))
                            else:
                                g.wait_ge(sPE.h, b[sPE] - 4 * 64)
                            src = wo_t[:, ocg * 256:(ocg + 1) * 256].rearrange("(c p) n -> p c n", p=P)
                            qdma(g, sQD, wqg[ocg % 2][:, :, :], src)

                    @block.tensor
                    def _(te, bi=bi, b=b):
                        te.wait_ge(sDV.h, b[sDV])
                        for ocg in range(8):
                            te.wait_ge(sQD.h, b[sQD] + 16 * (ocg + 1))
                            for oi in range(2):
                                oc = ocg * 2 + oi
                                par = oc % 2
                                if oc >= 2:
                                    te.wait_ge(sDV.h, b[sDV] + oc - 1)
                                for fc in range(FC):
                                    mm = te.matmul(PB[par][:, :], wqg[ocg % 2][:, fc, oi * P:(oi + 1) * P],
                                                   av_blk[:, fc, :],
                                                   start=(fc == 0), stop=(fc == FC - 1))
                                mm.then_inc(sPE.h, 1)
                    sPE.n += 16

                    @block.vector
                    def _(v, bi=bi, b=b):
                        for oc in range(16):
                            v.wait_ge(sPE.h, b[sPE] + oc + 1)
                            if oc >= 2:
                                v.wait_ge(sSD.h, b[sSD] + 16 * (oc - 1))
                            v.tensor_copy(ao_st[oc % 2][:, :], PB[oc % 2][:, :]).then_inc(sDV.h, 1)
                    sDV.n += 16

                    @block.scalar
                    def _(sc, bi=bi, b=b):
                        for oc in range(16):
                            sc.wait_ge(sDV.h, b[sDV] + oc + 1)
                            dst = ao_scr[oc * P:(oc + 1) * P, bi * 512:(bi + 1) * 512]
                            qdma(sc, sSD, dst, ao_st[oc % 2][:, :])

        s13.close()

        s58 = st.enter_context(contextlib.ExitStack())
        xn2 = s58.enter_context(nc.sbuf_tensor("xn2", [P, FC, T], BF16))
        w8t_sb = s58.enter_context(nc.sbuf_tensor("w8t_sb", [8, T], F32))
        s45 = st.enter_context(contextlib.ExitStack())
        R2m = s45.enter_context(nc.sbuf_tensor("R2m", [1, T], F32, side="right"))
        R2r = s45.enter_context(nc.sbuf_tensor("R2r", [1, T], F32, side="right"))
        R2mr = s45.enter_context(nc.sbuf_tensor("R2mr", [1, T], F32, side="right"))
        if PHASES >= 4:
            s4 = contextlib.ExitStack()
            with s4:
                aob = s4.enter_context(nc.sbuf_tensor("aob", [P, FC, 512], BF16))
                x1b = s4.enter_context(nc.sbuf_tensor("x1b", [P, FC, 512], F32))
                x1bf = s4.enter_context(nc.sbuf_tensor("x1bf", [P, FC, 512], BF16))
                Rq4 = s4.enter_context(nc.sbuf_tensor("Rq4", [1, 2, 512], F32, side="right"))
                Rmsq4 = s4.enter_context(nc.sbuf_tensor("Rmsq4", [1, 2, 512], F32, side="right"))
                Rsd4 = s4.enter_context(nc.sbuf_tensor("Rsd4", [1, 2, 512], F32, side="right"))
                b = snap()

                @block.sync
                def _(s):
                    wait_all(s, [sSD], b)
                    for i in range(4):
                        if i >= 1:
                            s.wait_ge(sSD.h, b[sSD] + 16 * i)
                            s.wait_ge(sAC.h, b[sAC] + 2 * (i - 1) + 1)
                        c0 = 64 + i * 512
                        src = x_slab[:, c0:c0 + 512].rearrange("(c p) t -> p c t", p=P)
                        qdma(s, sHD, x1b[:, :, :], src)

                @block.gpsimd
                def _(g):
                    wait_all(g, [sSD], b)
                    for i in range(4):
                        if i >= 1:
                            g.wait_ge(sDV.h, b[sDV] + 4 * (i - 1) + 1)
                        src = ao_scr[:, i * 512:(i + 1) * 512].rearrange("(c p) t -> p c t", p=P)
                        qdma(g, sQD, aob[:, :, :], src)

                @block.vector
                def _(v):
                    for i in range(4):
                        ip = i % 2
                        v.wait_ge(sHD.h, b[sHD] + 16 * (i + 1))
                        v.wait_ge(sQD.h, b[sQD] + 16 * (i + 1))
                        for fc in range(FC):
                            tt = v.tensor_tensor(x1b[:, fc, :], x1b[:, fc, :], aob[:, fc, :], op=ALU.add)
                        tt.then_inc(sDV.h, 1)
                        if i >= 1:
                            v.wait_ge(sPE.h, b[sPE] + 2 * i)
                        v.tensor_copy(x1bf[:, :, :], x1b[:, :, :]).then_inc(sDV.h, 1)
                        v.wait_ge(sPE.h, b[sPE] + 2 * i + 2)
                        if i >= 2:
                            v.wait_ge(sAC.h, b[sAC] + 2 * (i - 2) + 2)
                        v.tensor_scalar(R2m[:, i * 512:(i + 1) * 512], PB[0][0:1, :], INVD, None, op0=ALU.mult)
                        v.tensor_scalar(Rq4[:, ip, :], PB[1][0:1, :], INVD, None, op0=ALU.mult)
                        v.tensor_tensor(Rmsq4[:, ip, :], R2m[:, i * 512:(i + 1) * 512],
                                        R2m[:, i * 512:(i + 1) * 512], op=ALU.mult)
                        v.tensor_tensor(Rq4[:, ip, :], Rq4[:, ip, :], Rmsq4[:, ip, :],
                                        op=ALU.subtract)
                        v.tensor_scalar(Rq4[:, ip, :], Rq4[:, ip, :], EPS, None,
                                        op0=ALU.add).then_inc(sDV.h, 1)
                        v.wait_ge(sAC.h, b[sAC] + 2 * i + 2)
                        v.reciprocal(R2r[:, i * 512:(i + 1) * 512], Rsd4[:, ip, :])
                        v.tensor_tensor(R2mr[:, i * 512:(i + 1) * 512], R2m[:, i * 512:(i + 1) * 512],
                                        R2r[:, i * 512:(i + 1) * 512], op=ALU.mult).then_inc(sDV.h, 1)
                sDV.n += 4 * 4

                @block.scalar
                def _(sc):
                    for i in range(4):
                        ip = i % 2
                        sc.wait_ge(sPE.h, b[sPE] + 2 * i + 1)
                        sc.activation(x1bf[:, :, :], x1b[:, :, :], AF.Square).then_inc(sAC.h, 1)
                        sc.wait_ge(sDV.h, b[sDV] + 4 * i + 3)
                        sc.activation(Rsd4[:, ip, :], Rq4[:, ip, :], AF.Sqrt).then_inc(sAC.h, 1)
                        qdma(sc, sSD, out[:, i * 512:(i + 1) * 512].rearrange("(c p) t -> p c t", p=P),
                                     x1b[:, :, :])
                sAC.n += 2 * 4

                @block.tensor
                def _(te):
                    for i in range(4):
                        te.wait_ge(sDV.h, b[sDV] + 4 * i + 2)
                        if i >= 1:
                            te.wait_ge(sDV.h, b[sDV] + 4 * (i - 1) + 3)
                        for fc in range(FC):
                            mm = te.matmul(PB[0][0:1, :], ones_bf[:, :], x1bf[:, fc, :],
                                           start=(fc == 0), stop=(fc == FC - 1))
                        mm.then_inc(sPE.h, 1)
                        te.wait_ge(sAC.h, b[sAC] + 2 * i + 1)
                        for fc in range(FC):
                            mm = te.matmul(PB[1][0:1, :], ones_bf[:, :], x1bf[:, fc, :],
                                           start=(fc == 0), stop=(fc == FC - 1))
                        mm.then_inc(sPE.h, 1)
                sPE.n += 2 * 4

        if PHASES >= 5:
            s5 = contextlib.ExitStack()
            with s5:
                x1in = s5.enter_context(nc.sbuf_tensor("x1in", [P, FC, 512], F32))
                xnf = [s5.enter_context(nc.sbuf_tensor(f"xnf{i}", [P, FC, 512], F32)) for i in range(2)]
                t15 = s5.enter_context(nc.sbuf_tensor("t15", [P, 512], F32))
                l_sb = s5.enter_context(nc.sbuf_tensor("l_sb", [P, 128], F32))
                w8_sb = s5.enter_context(nc.sbuf_tensor("w8_sb", [P, 128], F32))
                e8 = s5.enter_context(nc.sbuf_tensor("e8", [P, 8], F32))
                p8 = s5.enter_context(nc.sbuf_tensor("p8", [P, 8], F32))
                sc8 = s5.enter_context(nc.sbuf_tensor("sc8", [P, 6], F32))
                rdbg = s5.enter_context(nc.sbuf_tensor("rdbg", [P, 32], F32))
                b = snap()

                @block.sync
                def _(s):
                    wait_all(s, [sSD], b)
                    for cc in range(4):
                        if cc >= 1:
                            s.wait_ge(sDV.h, b[sDV] + 42 * (cc - 1) + 1)
                        qdma(s, sHD, x1in[:, :, :],
                                    out[:, cc * 512:(cc + 1) * 512].rearrange("(c p) t -> p c t", p=P)
                                    )

                @block.tensor
                def _(te):
                    for cc in range(4):
                        par = cc % 2
                        te.wait_ge(sDV.h, b[sDV])
                        if cc >= 2:
                            te.wait_ge(sDV.h, b[sDV] + 42 * (cc - 2) + 1)
                        te.matmul(PB[0 + par][:, :], ones1_f[:, :], R2r[:, cc * 512:(cc + 1) * 512],
                                  start=True, stop=True)
                        mm = te.matmul(PB[2 + par][:, :], ones1_f[:, :], R2mr[:, cc * 512:(cc + 1) * 512],
                                       start=True, stop=True)
                        mm.then_inc(sPE.h, 1)
                        te.wait_ge(sDV.h, b[sDV] + 42 * cc + 1)
                        if cc >= 2:
                            te.wait_ge(sDV.h, b[sDV] + 42 * (cc - 2) + 2)
                        for j in range(4):
                            for fc in range(FC):
                                mm = te.matmul(PB[4 + par][:, j * 8:(j + 1) * 8],
                                               xnf[par][:, fc, j * P:(j + 1) * P],
                                               gate_sb[:, fc, :],
                                               start=(fc == 0), stop=(fc == FC - 1))
                        mm.then_inc(sPE.h, 1)
                sPE.n += 2 * 4

                @block.vector
                def _(v):
                    dvc = b[sDV]
                    for cc in range(4):
                        par = cc % 2
                        v.wait_ge(sHD.h, b[sHD] + 16 * (cc + 1))
                        v.wait_ge(sPE.h, b[sPE] + 2 * cc + 1)
                        for fc in range(FC):
                            v.tensor_tensor(t15[:, :], x1in[:, fc, :], PB[0 + par][:, :], op=ALU.mult)
                            tt = v.tensor_tensor(xnf[par][:, fc, :], t15[:, :], PB[2 + par][:, :],
                                                 op=ALU.subtract)
                        tt.then_inc(sDV.h, 1)
                        dvc += 1
                        v.tensor_copy(xn2[:, :, cc * 512:(cc + 1) * 512], xnf[par][:, :, :])
                        v.wait_ge(sPE.h, b[sPE] + 2 * cc + 2)
                        v.tensor_copy(l_sb[:, cc * 32:(cc + 1) * 32], PB[4 + par][:, 0:32]).then_inc(sDV.h, 1)
                        dvc += 1
                        for j in range(cc * 4, cc * 4 + 4):
                            lj = l_sb[:, j * 8:(j + 1) * 8]

                            def _st(instr):
                                nonlocal dvc
                                instr.then_inc(sDV.h, 1)
                                dvc += 1
                                v.wait_ge(sDV.h, dvc)
                            _st(v.reduce_max(sc8[:, 0:1], lj, axis=AX.X))
                            _st(v.reduce_max(sc8[:, 1:2], lj, axis=AX.X, negate=True))
                            _st(v.tensor_tensor(e8[:, :], lj, sc8[:, 0:1].broadcast_to([P, 8]),
                                                op=ALU.is_ge))
                            _st(v.scalar_tensor_tensor(p8[:, :], e8[:, :], -60.0, lj,
                                                       op0=ALU.mult, op1=ALU.add))
                            _st(v.reduce_max(sc8[:, 2:3], p8[:, :], axis=AX.X))
                            _st(v.tensor_tensor(e8[:, :], lj, sc8[:, 2:3].broadcast_to([P, 8]),
                                                op=ALU.is_ge))
                            v.wait_ge(sAC.h, b[sAC] + j + 1)
                            _st(v.reduce_sum(sc8[:, 3:4], p8[:, :], axis=AX.X))
                            _st(v.reciprocal(sc8[:, 4:5], sc8[:, 3:4]))
                            _st(v.tensor_tensor(p8[:, :], p8[:, :], sc8[:, 4:5].broadcast_to([P, 8]),
                                                op=ALU.mult))
                            _st(v.tensor_tensor(w8_sb[:, j * 8:(j + 1) * 8], p8[:, :], e8[:, :],
                                                op=ALU.mult))
                sDV.n += 4 * 42

                @block.scalar
                def _(sc):
                    for cc in range(4):
                        for j in range(cc * 4, cc * 4 + 4):
                            sc.wait_ge(sDV.h, b[sDV] + 42 * cc + 2 + (j - cc * 4) * 10 + 5)
                            sc.activation(p8[:, :], l_sb[:, j * 8:(j + 1) * 8], AF.Exp,
                                          bias=sc8[:, 1:2]).then_inc(sAC.h, 1)
                sAC.n += 16

                b2 = snap()

                @block.tensor
                def _(te):
                    wait_all(te, [sDV], b2)
                    for j in range(16):
                        if j >= 1:
                            te.wait_ge(sDV.h, b2[sDV] + j)
                        mm = te.transpose(PB[6][0:8, 0:128], w8_sb[:, j * 8:(j + 1) * 8], eye_sb[:, :])
                        mm.then_inc(sPE.h, 1)
                sPE.n += 16

                @block.vector
                def _(v):
                    for j in range(16):
                        v.wait_ge(sPE.h, b2[sPE] + j + 1)
                        v.tensor_copy(w8t_sb[0:8, j * P:(j + 1) * P], PB[6][0:8, 0:128]).then_inc(sDV.h, 1)
                sDV.n += 16

        s45.close()
        s67 = st.enter_context(contextlib.ExitStack())
        sigB_sb = s67.enter_context(nc.sbuf_tensor("sigB_sb", [P, T], F32, side="right"))

        if PHASES >= 6:
            s6 = contextlib.ExitStack()
            with s6:
                sig_row = s6.enter_context(nc.sbuf_tensor("sig_row", [1, T], F32))
                gub = [s6.enter_context(nc.sbuf_tensor(f"gub{i}", [P, FC, 256], BF16)) for i in range(2)]
                hs_st = [s6.enter_context(nc.sbuf_tensor(f"hs_st{i}", [P, T], BF16)) for i in range(2)]
                s_act = [s6.enter_context(nc.sbuf_tensor(f"s_act{i}", [P, 512], F32)) for i in range(2)]
                b = snap()

                if "sg" in KP6:
                    @block.tensor
                    def _(te):
                        wait_all(te, [sDV], b)
                        for cc in range(4):
                            par = cc % 2
                            if cc >= 2:
                                te.wait_ge(sAC.h, b[sAC] + cc - 1)
                            for fc in range(FC):
                                mm = te.matmul(PB[4 + par][0:1, :], sgw_sb[:, fc, :],
                                               xn2[:, fc, cc * 512:(cc + 1) * 512],
                                               start=(fc == 0), stop=(fc == FC - 1))
                            mm.then_inc(sPE.h, 1)
                        for cc in range(4):
                            te.wait_ge(sAC.h, b[sAC] + cc + 1)
                            if cc >= 1:
                                te.wait_ge(sDV.h, b[sDV] + cc)
                            mm = te.matmul(PB[6][:, :], ones1_f[:, :], sig_row[:, cc * 512:(cc + 1) * 512],
                                           start=True, stop=True)
                            mm.then_inc(sPE.h, 1)
                    sPE.n += 8

                    @block.scalar
                    def _(sc):
                        for cc in range(4):
                            par = cc % 2
                            sc.wait_ge(sPE.h, b[sPE] + cc + 1)
                            sc.activation(sig_row[:, cc * 512:(cc + 1) * 512], PB[4 + par][0:1, :],
                                          AF.Sigmoid).then_inc(sAC.h, 1)
                    sAC.n += 4

                    @block.vector
                    def _(v):
                        for cc in range(4):
                            v.wait_ge(sPE.h, b[sPE] + 4 + cc + 1)
                            v.tensor_copy(sigB_sb[:, cc * 512:(cc + 1) * 512], PB[6][:, :]).then_inc(sDV.h, 1)
                    sDV.n += 4

                GU = "gu" in KP6
                b = snap()
                NJ = KNJ

                if GU:
                  @block.gpsimd
                  def _(g):
                    for j in range(NJ):
                        if j >= 2:
                            g.wait_ge(sPE.h, b[sPE] + 4 * (j - 1))
                        else:
                            g.wait_ge(sPE.h, b[sPE])
                        src1 = gu_t[:, j * 256:(j + 1) * 256].rearrange("(c p) n -> p c n", p=P)
                        qdma(g, sQD, gub[j % 2][:, :, :], src1)

                if GU:
                  @block.tensor
                  def _(te):
                    for j in range(NJ):
                        te.wait_ge(sQD.h, b[sQD] + 16 * (j + 1))
                        for tc in range(4):
                            gi = j * 4 + tc
                            par = gi % 2
                            if gi >= 2:
                                te.wait_ge(sAC.h, b[sAC] + gi - 1)
                                te.wait_ge(sDV.h, b[sDV] + gi - 1)
                            for fc in range(FC):
                                te.matmul(PB[0 + par][:, :], gub[j % 2][:, fc, 0:P],
                                          xn2[:, fc, tc * 512:(tc + 1) * 512],
                                          start=(fc == 0), stop=(fc == FC - 1))
                            for fc in range(FC):
                                mm = te.matmul(PB[2 + par][:, :], gub[j % 2][:, fc, P:256],
                                               xn2[:, fc, tc * 512:(tc + 1) * 512],
                                               start=(fc == 0), stop=(fc == FC - 1))
                            mm.then_inc(sPE.h, 1)
                if GU:
                    sPE.n += 4 * NJ

                if GU:
                  @block.scalar
                  def _(sc):
                    for j in range(NJ):
                        for tc in range(4):
                            gi = j * 4 + tc
                            par = gi % 2
                            sc.wait_ge(sPE.h, b[sPE] + gi + 1)
                            if gi >= 2:
                                sc.wait_ge(sDV.h, b[sDV] + gi - 1)
                            sc.activation(s_act[par][:, :], PB[0 + par][:, :],
                                          AF.Silu).then_inc(sAC.h, 1)
                        sc.wait_ge(sDV.h, b[sDV] + 4 * (j + 1))
                        qdma(sc, sSD, hs_scr[j * P:(j + 1) * P, :], hs_st[j % 2][:, :])
                if GU:
                    sAC.n += 4 * NJ

                if GU:
                  @block.vector
                  def _(v):
                    for j in range(NJ):
                        for tc in range(4):
                            gi = j * 4 + tc
                            par = gi % 2
                            v.wait_ge(sAC.h, b[sAC] + gi + 1)
                            if j >= 2 and tc == 0:
                                v.wait_ge(sSD.h, b[sSD] + 16 * (j - 1))
                            v.tensor_tensor(hs_st[j % 2][:, tc * 512:(tc + 1) * 512], PB[2 + par][:, :],
                                            s_act[par][:, :], op=ALU.mult).then_inc(sDV.h, 1)
                if GU:
                    sDV.n += 4 * NJ



        if PHASES >= 7:
            s7 = contextlib.ExitStack()
            with s7:
                hs_sl = s7.enter_context(nc.sbuf_tensor("hs_sl", [P, 44, 1024], BF16))
                dnb = [s7.enter_context(nc.sbuf_tensor(f"dnb{i}", [P, 44, P], BF16)) for i in range(2)]
                o_t = [s7.enter_context(nc.sbuf_tensor(f"o_t{i}", [P, 512], F32)) for i in range(4)]
                tmp7 = [s7.enter_context(nc.sbuf_tensor(f"tmp7{i}", [P, 512], F32)) for i in range(2)]
                b7 = snap()
                for hf in range(2):
                    b = snap()

                    @block.sync
                    def _(s, hf=hf, b=b, b7=b7):
                        s.wait_ge(sSD.h, b7[sSD])
                        s.wait_ge(sPE.h, b[sPE])
                        for piece in range(4):
                            src = hs_scr[piece * 1408:(piece + 1) * 1408, hf * 1024:(hf + 1) * 1024]
                            src = src.rearrange("(c p) t -> p c t", p=P)
                            qdma(s, sHD, hs_sl[:, piece * 11:(piece + 1) * 11, :], src)

                    @block.gpsimd
                    def _(g, hf=hf, b=b):
                        for oc in range(16):
                            if oc >= 2:
                                g.wait_ge(sPE.h, b[sPE] + 2 * (oc - 1))
                            else:
                                g.wait_ge(sPE.h, b[sPE])
                            src = dn_t[:, oc * P:(oc + 1) * P].rearrange("(c p) n -> p c n", p=P)
                            qdma(g, sQD, dnb[oc % 2][:, :, :], src)

                    @block.tensor
                    def _(te, hf=hf, b=b):
                        te.wait_ge(sHD.h, b[sHD] + 16 * 4)
                        for oc in range(16):
                            te.wait_ge(sQD.h, b[sQD] + 16 * (oc + 1))
                            for tc2 in range(2):
                                gi = oc * 2 + tc2
                                par = gi % 4
                                if gi >= 4:
                                    te.wait_ge(sDV.h, b[sDV] + 2 * (gi - 3) - 1)
                                for mc in range(44):
                                    mm = te.matmul(PB[par][:, :], dnb[oc % 2][:, mc, :],
                                                   hs_sl[:, mc, tc2 * 512:(tc2 + 1) * 512],
                                                   start=(mc == 0), stop=(mc == 43))
                                mm.then_inc(sPE.h, 1)
                    sPE.n += 32

                    @block.sync
                    def _(s, hf=hf, b=b):
                        wait_all(s, [sSD], b)
                        for gi in range(32):
                            oc, tc2 = divmod(gi, 2)
                            if gi >= 4:
                                s.wait_ge(sSD.h, b[sSD] + 16 * (gi - 3))
                            c0 = hf * 1024 + tc2 * 512
                            qdma(s, sHD, o_t[gi % 4][:, :], out[oc * P:(oc + 1) * P, c0:c0 + 512]
                                        )

                    @block.vector
                    def _(v, hf=hf, b=b):
                        for gi in range(32):
                            oc, tc2 = divmod(gi, 2)
                            par = gi % 4
                            c0 = hf * 1024 + tc2 * 512
                            v.wait_ge(sPE.h, b[sPE] + gi + 1)
                            v.tensor_tensor(tmp7[gi % 2][:, :], PB[par][:, :], sigB_sb[:, c0:c0 + 512],
                                            op=ALU.mult).then_inc(sDV.h, 1)
                            v.wait_ge(sHD.h, b[sHD] + 16 * (4 + gi + 1))
                            v.tensor_tensor(o_t[gi % 4][:, :], o_t[gi % 4][:, :], tmp7[gi % 2][:, :],
                                            op=ALU.add).then_inc(sDV.h, 1)
                    sDV.n += 64

                    @block.scalar
                    def _(sc, hf=hf, b=b):
                        for gi in range(32):
                            oc, tc2 = divmod(gi, 2)
                            c0 = hf * 1024 + tc2 * 512
                            sc.wait_ge(sDV.h, b[sDV] + 2 * gi + 2)
                            qdma(sc, sSD, out[oc * P:(oc + 1) * P, c0:c0 + 512], o_t[gi % 4][:, :]
                                         )

        s67.close()

        if PHASES >= 8:
            s8 = contextlib.ExitStack()
            with s8:
                comb = s8.enter_context(nc.sbuf_tensor("comb", [P, FC, 1024], F32))
                mid = s8.enter_context(nc.sbuf_tensor("mid", [P, 11, 1024], BF16))
                w13b = [s8.enter_context(nc.sbuf_tensor(f"w13b{i}", [P, FC, 256], BF16)) for i in range(2)]
                w2b = [s8.enter_context(nc.sbuf_tensor(f"w2b{i}", [P, 11, P], BF16)) for i in range(2)]
                wB = s8.enter_context(nc.sbuf_tensor("wB", [P, 1024], F32))
                wrow = s8.enter_context(nc.sbuf_tensor("wrow", [1, 1024], F32))
                sact8 = [s8.enter_context(nc.sbuf_tensor(f"sact8{i}", [P, 512], F32, side="right")) for i in range(2)]
                uw8 = [s8.enter_context(nc.sbuf_tensor(f"uw8{i}", [P, 512], F32, side="right")) for i in range(2)]
                o_t8 = [s8.enter_context(nc.sbuf_tensor(f"o_t8{i}", [P, 512], F32)) for i in range(4)]
                for hf in range(2):
                    bh = snap()

                    @block.vector
                    def _(v, hf=hf, bh=bh):
                        wait_all(v, [sDV, sPE], bh)
                        v.memset(comb[:, :, :], 0.0).then_inc(sDV.h, 1)
                    sDV.n += 1

                    for e in range(NEXP):
                        b = snap()
                        @block.gpsimd
                        def _(g, e=e, hf=hf, b=b):
                            if PHASES >= 5:
                                g.wait_ge(sDV.h, b[sDV])
                            qdma(g, sQD, wrow[:, 0:1024], w8t_sb[e:e + 1, hf * 1024:(hf + 1) * 1024]
                                        )

                        @block.tensor
                        def _(te, e=e, b=b):
                            te.wait_ge(sQD.h, b[sQD] + 16)
                            if e >= 1 or True:
                                te.wait_ge(sDV.h, b[sDV])
                            te.matmul(PB[6][:, :], ones1_f[:, :], wrow[:, 0:512], start=True, stop=True)
                            mm = te.matmul(PB[7][:, :], ones1_f[:, :], wrow[:, 512:1024], start=True, stop=True)
                            mm.then_inc(sPE.h, 1)
                        sPE.n += 1

                        @block.vector
                        def _(v, e=e, b=b):
                            v.wait_ge(sPE.h, b[sPE] + 1)
                            v.tensor_copy(wB[:, 0:512], PB[6][:, :])
                            v.tensor_copy(wB[:, 512:1024], PB[7][:, :]).then_inc(sDV.h, 1)
                        sDV.n += 1

                        b = snap()

                        @block.gpsimd
                        def _(g, e=e, b=b):
                            for mc in range(11):
                                if mc >= 2:
                                    g.wait_ge(sPE.h, b[sPE] + 2 * (mc - 1))
                                src1 = w13_t[e, :, mc * 256:(mc + 1) * 256].rearrange("(c p) n -> p c n", p=P)
                                qdma(g, sQD, w13b[mc % 2][:, :, :], src1)

                        @block.tensor
                        def _(te, e=e, hf=hf, b=b):
                            for mc in range(11):
                                te.wait_ge(sQD.h, b[sQD] + 16 * (mc + 1))
                                for tc in range(2):
                                    gi = mc * 2 + tc
                                    par = gi % 2
                                    if gi >= 2:
                                        te.wait_ge(sAC.h, b[sAC] + gi - 1)
                                        te.wait_ge(sDV.h, b[sDV] + 2 * (gi - 1))
                                    c0 = hf * 1024 + tc * 512
                                    for fc in range(FC):
                                        te.matmul(PB[0 + par][:, :], w13b[mc % 2][:, fc, 0:P],
                                                  xn2[:, fc, c0:c0 + 512],
                                                  start=(fc == 0), stop=(fc == FC - 1))
                                    for fc in range(FC):
                                        mm = te.matmul(PB[2 + par][:, :], w13b[mc % 2][:, fc, P:256],
                                                       xn2[:, fc, c0:c0 + 512],
                                                       start=(fc == 0), stop=(fc == FC - 1))
                                    mm.then_inc(sPE.h, 1)
                        sPE.n += 22

                        @block.scalar
                        def _(sc, e=e, b=b):
                            for gi in range(22):
                                par = gi % 2
                                sc.wait_ge(sPE.h, b[sPE] + gi + 1)
                                if gi >= 2:
                                    sc.wait_ge(sDV.h, b[sDV] + 2 * gi - 2)
                                sc.activation(sact8[par][:, :], PB[0 + par][:, :], AF.Silu).then_inc(sAC.h, 1)
                        sAC.n += 22

                        @block.vector
                        def _(v, e=e, hf=hf, b=b):
                            for gi in range(22):
                                mc, tc = divmod(gi, 2)
                                par = gi % 2
                                v.wait_ge(sAC.h, b[sAC] + gi + 1)
                                v.tensor_tensor(uw8[par][:, :], PB[2 + par][:, :],
                                                wB[:, tc * 512:(tc + 1) * 512], op=ALU.mult)
                                v.tensor_tensor(mid[:, mc, tc * 512:(tc + 1) * 512], uw8[par][:, :],
                                                sact8[par][:, :], op=ALU.mult).then_inc(sDV.h, 2)
                        sDV.n += 44

                        b = snap()

                        @block.gpsimd
                        def _(g, e=e, b=b):
                            for oc in range(16):
                                if oc >= 2:
                                    g.wait_ge(sPE.h, b[sPE] + 2 * (oc - 1))
                                src = w2_t[e, :, oc * P:(oc + 1) * P].rearrange("(c p) n -> p c n", p=P)
                                qdma(g, sQD, w2b[oc % 2][:, :, :], src)

                        @block.tensor
                        def _(te, e=e, b=b):
                            te.wait_ge(sDV.h, b[sDV])
                            for oc in range(16):
                                te.wait_ge(sQD.h, b[sQD] + 16 * (oc + 1))
                                for tc in range(2):
                                    gi = oc * 2 + tc
                                    par = 4 + gi % 2
                                    if gi >= 2:
                                        te.wait_ge(sDV.h, b[sDV] + gi - 1)
                                    for mc in range(11):
                                        mm = te.matmul(PB[par][:, :], w2b[oc % 2][:, mc, :],
                                                       mid[:, mc, tc * 512:(tc + 1) * 512],
                                                       start=(mc == 0), stop=(mc == 10))
                                    mm.then_inc(sPE.h, 1)
                        sPE.n += 32

                        @block.vector
                        def _(v, e=e, b=b):
                            for gi in range(32):
                                oc, tc = divmod(gi, 2)
                                par = 4 + gi % 2
                                v.wait_ge(sPE.h, b[sPE] + gi + 1)
                                v.tensor_tensor(comb[:, oc, tc * 512:(tc + 1) * 512],
                                                comb[:, oc, tc * 512:(tc + 1) * 512],
                                                PB[par][:, :], op=ALU.add).then_inc(sDV.h, 1)
                        sDV.n += 32

                    b = snap()

                    @block.sync
                    def _(s, hf=hf, b=b):
                        wait_all(s, [sSD, sDV], b)
                        for gi in range(32):
                            oc, tc = divmod(gi, 2)
                            if gi >= 4:
                                s.wait_ge(sSD.h, b[sSD] + 16 * (gi - 3))
                            c0 = hf * 1024 + tc * 512
                            qdma(s, sHD, o_t8[gi % 4][:, :], out[oc * P:(oc + 1) * P, c0:c0 + 512]
                                        )

                    @block.vector
                    def _(v, hf=hf, b=b):
                        for gi in range(32):
                            oc, tc = divmod(gi, 2)
                            v.wait_ge(sHD.h, b[sHD] + 16 * (gi + 1))
                            v.tensor_tensor(o_t8[gi % 4][:, :], o_t8[gi % 4][:, :],
                                            comb[:, oc, tc * 512:(tc + 1) * 512], op=ALU.add).then_inc(sDV.h, 1)
                    sDV.n += 32

                    @block.scalar
                    def _(sc, hf=hf, b=b):
                        for gi in range(32):
                            oc, tc = divmod(gi, 2)
                            c0 = hf * 1024 + tc * 512
                            sc.wait_ge(sDV.h, b[sDV] + gi + 1)
                            qdma(sc, sSD, out[oc * P:(oc + 1) * P, c0:c0 + 512], o_t8[gi % 4][:, :]
                                         )

        if KDBG == "k":
            fin0 = snap()

            @block.gpsimd
            def _(g):
                wait_all(g, ALL, fin0)
                qdma(g, sQD, out[0:1024, :].rearrange("(c p) t -> p c t", p=P),
                     k_sb[:, :, 64:64 + T])

        if KDBG == "v":
            fin0 = snap()

            @block.gpsimd
            def _(g):
                wait_all(g, ALL, fin0)
                qdma(g, sQD, dbg.rearrange("(tt p) o -> p tt o", p=P), vT_sb[:, :, :])

        if KDBG == "rt":
            fin0 = snap()

            @block.gpsimd
            def _(g):
                wait_all(g, ALL, fin0)
                qdma(g, sQD, dbg[128:256, 0:32], rdbg[:, :])

        if KDBG == "w8":
            fin0 = snap()

            @block.gpsimd
            def _(g):
                wait_all(g, ALL, fin0)
                qdma(g, sQD, dbg[0:8, 0:1024], w8t_sb[0:8, 0:1024])
                qdma(g, sQD, dbg[8:16, 0:1024], w8t_sb[0:8, 1024:2048])

        if KDBG == "xn1":
            fin0 = snap()

            @block.gpsimd
            def _(g):
                wait_all(g, ALL, fin0)
                qdma(g, sQD, out.rearrange("(c p) t -> p c t", p=P),
                            xn1[:, :, 64:64 + T])
        if KDBG == "ao":
            fin0 = snap()

            @block.gpsimd
            def _(g):
                wait_all(g, ALL, fin0)
                qdma(g, sQD, out, ao_scr)

        fin = snap()

        @block.sync
        def _(s):
            s.wait_ge(sSD.h, fin[sSD])
            s.wait_ge(sQD.h, fin[sQD])

        @block.gpsimd
        def _(g):
            g.wait_ge(sSD.h, fin[sSD])

    return nc


def _masks():
    r = np.arange(P)[None, :]
    cb = np.arange(P)[:, None]
    mm = np.zeros((P, 2, P), np.float32)
    mm[:, 0, :] = ((cb >= r + 57) & (cb <= r + 64)).astype(np.float32)
    cbB = (128 + np.arange(P))[:, None]
    bm = ((cbB >= r + 57) & (cbB <= r + 64)).astype(np.float32)
    mm[0:64, 1, :] = bm[0:64]
    mmi = np.ones((4, P), np.float32)

    mf = np.zeros((P, 2, P), np.float32)
    j = cb - 64
    mf[:, 0, :] = ((j >= 4) & (j <= r) & (r - j < 8)).astype(np.float32)
    jB = cbB - 64
    bf = ((jB >= 4) & (jB <= r) & (r - jB < 8)).astype(np.float32)
    mf[0:64, 1, :] = bf[0:64]
    jj = np.arange(4)[:, None]
    mfi = (jj <= r).astype(np.float32)
    return (np.ascontiguousarray(mf.reshape(P, 256)),
            np.ascontiguousarray(mm.reshape(P, 256)),
            np.ascontiguousarray(mfi), np.ascontiguousarray(mmi))


def kernel(L0_attn_x, L0_attn_Wq, L0_attn_Wk, L0_attn_Wv, L0_attn_Wo,
           L0_ffn_gate_w, L0_ffn_shared_gate_w, L0_ffn_shared_gate_up_w,
           L0_ffn_shared_down_w, L0_ffn_w13, L0_ffn_w2):
    x = np.asarray(L0_attn_x, dtype=np.float32)
    cw = lambda a: np.ascontiguousarray(np.asarray(a, dtype=np.float32))
    wq_t = cw(np.asarray(L0_attn_Wq).T)
    wk_t = cw(np.asarray(L0_attn_Wk).T)
    wv_t = cw(np.asarray(L0_attn_Wv).T)
    wo_t = cw(np.asarray(L0_attn_Wo).T)
    gate_t = cw(np.asarray(L0_ffn_gate_w).T)
    sg_t = cw(np.asarray(L0_ffn_shared_gate_w).T)
    gu_t = np.asarray(L0_ffn_shared_gate_up_w, dtype=np.float32).T
    gu_t = cw(gu_t.reshape(2048, 2, 44, 128).transpose(0, 2, 1, 3).reshape(2048, 11264))
    dn_t = cw(np.asarray(L0_ffn_shared_down_w).T)
    w13 = np.asarray(L0_ffn_w13, dtype=np.float32)
    w2 = np.asarray(L0_ffn_w2, dtype=np.float32)
    w13_t = np.transpose(w13, (0, 2, 1)).astype(np.float32)
    w13_t = cw(w13_t.reshape(8, 2048, 2, 11, 128).transpose(0, 1, 3, 2, 4).reshape(8, 2048, 2816))
    w2_t = cw(np.transpose(w2, (0, 2, 1)))
    gate_pt = cw(gate_t.reshape(FC, P, 8).transpose(1, 0, 2))
    sg_pt = cw(sg_t.reshape(FC, P, 1).transpose(1, 0, 2))
    eye = np.eye(P, dtype=np.float32)
    mF, mM, mFI, mMI = _masks()

    nc = build_nc()

    in_maps = []
    for c in range(8):
        bb, si = divmod(c, 4)
        s0 = si * T
        xf = x[bb].T
        pre = xf[:, s0 - PRE:s0] if s0 > 0 else np.zeros((D, PRE), np.float32)
        slab = np.concatenate([pre, xf[:, s0:s0 + T], xf[:, 0:4],
                               np.zeros((D, W - T - PRE - 4), np.float32)], axis=1)
        in_maps.append(dict(
            x_slab=np.ascontiguousarray(slab), wq_t=wq_t, wk_t=wk_t, wv_t=wv_t, wo_t=wo_t,
            gate_pt=gate_pt, sg_pt=sg_pt, gu_t=gu_t, dn_t=dn_t, w13_t=w13_t, w2_t=w2_t,
            maskF=(mF if s0 == 0 else mM), maskM=mM,
            maskFI=(mFI if s0 == 0 else mMI), maskMI=mMI, eye_in=eye))

    res = run_bass_kernel_spmd(nc, in_maps, core_ids=list(range(8)))
    if KDBG:
        global _DBG_RES
        _DBG_RES = res

    full = np.empty((2, 8192, D), np.float32)
    for c in range(8):
        bb, si = divmod(c, 4)
        s0 = si * T
        full[bb, s0:s0 + T, :] = res.results[c]["out"].T
    return full
